# version 20
# speedup vs baseline: 1.3318x; 1.0170x over previous
import contextlib
import numpy as np
import concourse.bass as bass
import concourse.mybir as mybir
from concourse.bass_utils import run_bass_kernel_spmd

F32 = mybir.dt.float32
BF16 = mybir.dt.bfloat16
AX = mybir.AxisListType.X
AXY = mybir.AxisListType.XY
ALU = mybir.AluOpType
ACT = mybir.ActivationFunctionType

NCORES = 8
HID = 1024
HEADS = 16
DH = 64
SEQ = 16384
T = SEQ // NCORES
NBLK = T // 128
NCH = HID // 128
EPS = 1e-5
FF = 4096
NMCH = FF // 128

A_QBF = 0
A_WK = 0
A_WV = 16384
A_KTM = 32768
A_HPC = 32768
A_VTM = 36864
A_ATM = 36864
A_LN1TM = 40960
A_KVF = 40960
A_WQ = 45184
A_GFM = 53504
A_NUM = 61696
A_CBB = 65792
A_BOB = 67840
A_KVBF = 69888
A_WH = 72000
A_WO = 88384
A_UG = 0
A_YFM = 65536
A_W1R = 81920
A_W2R = 90112
A_BYTES = 106496
DEBUG = False


class Ctr:
    def __init__(self, es, nc, name):
        self.h = es.enter_context(nc.semaphore(name))
        self.n = 0

    def inc(self, instr, k=1):
        instr.then_inc(self.h, k)
        self.n += k
        return self.n

    def reserve(self, k=1):
        self.n += k
        return self.n


class DeferredScalar:

    def __init__(self):
        self.fns = []

    def add(self, fn):
        self.fns.append(fn)

    def emit(self, block):
        fns, self.fns = self.fns, []
        if fns:
            def body(sc):
                for f in fns:
                    f(sc)
            block.scalar(body)


def build_nc():
    nc = bass.Bass("TRN2", target_bir_lowering=False, debug=False)

    x_d = nc.dram_tensor("x", [T, HID], F32, kind="ExternalInput")
    wq_d = nc.dram_tensor("wq", [HID, HID], F32, kind="ExternalInput")
    wk_d = nc.dram_tensor("wk", [HID, HID], F32, kind="ExternalInput")
    wv_d = nc.dram_tensor("wv", [HID, HID], F32, kind="ExternalInput")
    wh_d = nc.dram_tensor("wh", [HID, HID], F32, kind="ExternalInput")
    wo_d = nc.dram_tensor("wo", [HID, HID], F32, kind="ExternalInput")
    bh_d = nc.dram_tensor("bh", [1, HID], F32, kind="ExternalInput")
    beta_d = nc.dram_tensor("beta", [1, HID], F32, kind="ExternalInput")
    bo_d = nc.dram_tensor("bo", [1, HID], F32, kind="ExternalInput")
    wrt_d = nc.dram_tensor("wrt", [HID, 8], F32, kind="ExternalInput")
    w1_d = nc.dram_tensor("w1", [HID, FF], F32, kind="ExternalInput")
    w2_d = nc.dram_tensor("w2", [FF, HID], F32, kind="ExternalInput")
    idn_d = nc.dram_tensor("idn_h", [128, 128], F32, kind="ExternalInput")
    out_d = nc.dram_tensor("out", [T, HID], F32, kind="ExternalOutput")
    dbg = {}
    if DEBUG:
        dbg["d_kv"] = nc.dram_tensor("d_kv", [64, HEADS * 65], F32, kind="ExternalOutput")
        dbg["d_kvbf"] = nc.dram_tensor("d_kvbf", [128, HEADS * 65], BF16, kind="ExternalOutput")
        dbg["d_ln1"] = nc.dram_tensor("d_ln1", [128, NCH * T], BF16, kind="ExternalOutput")
        dbg["d_q"] = nc.dram_tensor("d_q", [128, NCH * T], BF16, kind="ExternalOutput")
        dbg["d_attn0"] = nc.dram_tensor("d_attn0", [128, 1040], F32, kind="ExternalOutput")
        dbg["d_r2"] = nc.dram_tensor("d_r2", [128, NBLK * HID], F32, kind="ExternalOutput")
        dbg["d_p7"] = nc.dram_tensor("d_p7", [128, NBLK], F32, kind="ExternalOutput")
        dbg["d_y"] = nc.dram_tensor("d_y", [128, NCH * 1024], BF16, kind="ExternalOutput")
        dbg["d_hpc0"] = nc.dram_tensor("d_hpc0", [128, 1024], BF16, kind="ExternalOutput")
        dbg["d_q2"] = nc.dram_tensor("d_q2", [128, 2048], BF16, kind="ExternalOutput")
        dbg["d_cb"] = nc.dram_tensor("d_cb", [128, 1024], BF16, kind="ExternalOutput")
        dbg["d_g"] = nc.dram_tensor("d_g", [128, 4096], BF16, kind="ExternalOutput")
        dbg["d_ao0"] = nc.dram_tensor("d_ao0", [128, 1024], F32, kind="ExternalOutput")
        dbg["d_gtm0"] = nc.dram_tensor("d_gtm0", [128, 1024], BF16, kind="ExternalOutput")
        dbg["d_lnat0"] = nc.dram_tensor("d_lnat0", [128, 1040], F32, kind="ExternalOutput")
        dbg["d_stat0"] = nc.dram_tensor("d_stat0", [128, 8], F32, kind="ExternalOutput")
        dbg["d_wq0"] = nc.dram_tensor("d_wq0", [128, 1024], BF16, kind="ExternalOutput")
    cc_in = nc.dram_tensor("cc_in", [64, HEADS * 65], F32, kind="Internal")
    cc_out = nc.dram_tensor("cc_out", [64, HEADS * 65], F32, kind="Internal")

    es = contextlib.ExitStack()
    x_tm = es.enter_context(nc.sbuf_tensor("x_tm", [128, NBLK * HID], F32))
    lnx_fm = es.enter_context(nc.sbuf_tensor("lnx_fm", [128, NCH * T], BF16))
    arena = es.enter_context(nc.sbuf_tensor("arena", [128, A_BYTES // 4], F32))
    idn = es.enter_context(nc.sbuf_tensor("idnf", [128, 128], F32))
    idn_bf = es.enter_context(nc.sbuf_tensor("idnb", [128, 128], BF16))
    wrt_bf = es.enter_context(nc.sbuf_tensor("wrtb", [128, 64], BF16))
    scr2 = es.enter_context(nc.sbuf_tensor("scr2", [128, 512], F32))
    eT_sb = es.enter_context(nc.sbuf_tensor("eT_sb", [128, 48], F32))
    p7_tm = es.enter_context(nc.sbuf_tensor("p7_tm", [128, NBLK], F32))
    sq_scr = es.enter_context(nc.sbuf_tensor("sq_scr", [128, HID], BF16))
    stat = es.enter_context(nc.sbuf_tensor("stat", [128, 8], F32))
    sp1 = es.enter_context(nc.sbuf_tensor("sp1", [128, 96], F32))
    ones_bf = es.enter_context(nc.sbuf_tensor("ones_bf", [128, 1], BF16))
    eps_t = es.enter_context(nc.sbuf_tensor("eps_t", [128, 1], F32))
    nh_t = es.enter_context(nc.sbuf_tensor("nh_t", [128, 1], F32))

    S_MU, S_SS, S_VAR, S_S, S_NEGMS, S_T0, S_T1 = 0, 1, 2, 3, 4, 5, 6

    pp = [es.enter_context(nc.psum_tensor(f"pp{i}", [128, 512], F32)) for i in range(8)]

    def ap_f32(off, rows, cols):
        return arena[0:rows, off // 4: off // 4 + cols]

    def ap_bf16(off, rows, cols):
        return arena[:, off // 4: off // 4 + (cols + 1) // 2].bitcast(BF16)[0:rows, 0:cols]

    def wk_bf(c):
        return ap_bf16(A_WK + c * 2048, 128, 1024)

    def wv_bf(c):
        return ap_bf16(A_WV + c * 2048, 128, 1024)

    def wh_bf(c):
        return ap_bf16(A_WH + c * 2048, 128, 1024)

    def wq_bf(c):
        return ap_bf16(A_WQ + c * 2048, 128, 1024)

    def wo_bf(c):
        return ap_bf16(A_WO + c * 2048, 128, 1024)

    def ktm(i):
        return ap_bf16(A_KTM + (i % 2) * 2048, 128, 1024)

    def vtm(i):
        return ap_bf16(A_VTM + (i % 2) * 2048, 128, 1024)

    def ln1tm(i):
        return ap_bf16(A_LN1TM + (i % 2) * 2048, 128, 1024)

    def qbf():
        return ap_bf16(A_QBF, 128, 16384)

    _HPC = [A_HPC, A_HPC + 2048, A_NUM, A_NUM + 2048]

    def hpc(b):
        return ap_bf16(_HPC[b % 4], 128, 1024)

    def atm():
        return ap_f32(A_ATM, 128, 4160)

    def gfm():
        return ap_bf16(A_GFM, 128, 4096)

    def numsl(j):
        return ap_f32(A_NUM + (j % 2) * 2048, 128, 512)

    def gtm(b):
        if b % 2 == 0:
            return sq_scr[:, 0:1024]
        return scr2[:].bitcast(BF16)[0:128, 0:1024]

    def e_sb():
        return scr2[0:8, :]

    cb_b = ap_bf16(A_CBB, 128, 1024)
    bo_b = ap_bf16(A_BOB, 128, 1024)
    kvbf = ap_bf16(A_KVBF, 128, 1040)
    kvf32 = ap_f32(A_KVF, 64, 1040)
    bias_t1 = ap_f32(A_ATM, 128, 1024)
    bias_t2 = ap_f32(A_ATM + 4096, 128, 1024)

    def ug(m):
        return ap_bf16(A_UG + m * 2048, 128, 1024)

    def yfm(j):
        return ap_bf16(A_YFM + j * 2048, 128, 1024)

    def w1r(m):
        return ap_bf16(A_W1R + (m % 4) * 2048, 128, 1024)

    def w2r(j):
        return ap_bf16(A_W2R + (j % 2) * 8192, 128, 4096)

    rc_sb = es.enter_context(nc.sbuf_tensor("rc_sb", [128, 64], F32))
    c_te = Ctr(es, nc, "c_te")
    c_ve = Ctr(es, nc, "c_ve")
    c_se = Ctr(es, nc, "c_se")
    s_sd = Ctr(es, nc, "s_sd")
    s_x = Ctr(es, nc, "s_x")
    s_st = Ctr(es, nc, "s_st")
    s_wl = Ctr(es, nc, "s_wl")
    s_sq = Ctr(es, nc, "s_sq")
    s_cc = Ctr(es, nc, "s_cc")
    s_dbg = Ctr(es, nc, "s_dbg")

    lnc = None

    with nc.Block() as block:
        lnc_view = lnx_fm[:].rearrange("p (c n) -> p c n", c=NCH)

        X_Q = []

        def init_sync(sync):
            xr = x_d.ap().rearrange("(t p) k -> p t k", p=128)
            xo = x_tm[:].rearrange("p (t k) -> p t k", t=NBLK)
            for qtr in range(4):
                s_x.inc(sync.dma_start(out=xo[:, qtr * 4:(qtr + 1) * 4, :],
                                       in_=xr[:, qtr * 4:(qtr + 1) * 4, :]), 16)
                X_Q.append(s_x.n)
            s_x.inc(sync.dma_start(out=idn[:], in_=idn_d.ap()), 16)
        block.sync(init_sync)
        X_LOADED = s_x.n

        def init_gp(gp):
            s_wl.inc(gp.dma_start(out=bias_t1, in_=bh_d.ap().partition_broadcast(128)), 16)
            s_wl.inc(gp.dma_start(out=bias_t2, in_=beta_d.ap().partition_broadcast(128)), 16)
            s_wl.inc(gp.dma_start(out=bo_b, in_=bo_d.ap().partition_broadcast(128)), 16)
            s_wl.inc(gp.dma_start(
                out=ap_bf16(A_WK, 128, NCH * 1024).rearrange("p (c m) -> p c m", c=NCH),
                in_=wk_d.ap().rearrange("(c k) m -> k c m", c=NCH)), 16)
            s_wl.inc(gp.dma_start(
                out=ap_bf16(A_WV, 128, NCH * 1024).rearrange("p (c m) -> p c m", c=NCH),
                in_=wv_d.ap().rearrange("(c k) m -> k c m", c=NCH)), 16)
            s_wl.inc(gp.dma_start(
                out=ap_bf16(A_WH, 128, NCH * 1024).rearrange("p (c m) -> p c m", c=NCH),
                in_=wh_d.ap().rearrange("(c k) m -> k c m", c=NCH)), 16)
            gp.wait_ge(s_x.h, X_LOADED)
            s_wl.inc(gp.dma_start(out=idn_bf[:], in_=idn[:]), 16)
            s_wl.inc(gp.dma_start(
                out=wrt_bf[:].rearrange("p (c e) -> p c e", c=NCH),
                in_=wrt_d.ap().rearrange("(c k) e -> k c e", c=NCH)), 16)
        block.gpsimd(init_gp)
        BIAS_LOADED = 32
        P1W_LOADED = s_wl.n

        def init_vec(vec):
            vec.memset(ones_bf[:], 1.0)
            vec.memset(eps_t[:], EPS)
            vec.memset(nh_t[:], -0.5)
            vec.wait_ge(s_wl.h, P1W_LOADED)
            c_ve.inc(vec.tensor_tensor(cb_b, bias_t1, bias_t2, ALU.add))
        block.vector(init_vec)
        CB_DONE = c_ve.n

        def p1_pre_v(vec):
            vec.wait_ge(s_x.h, X_LOADED)
        block.vector(p1_pre_v)

        def p1_pre_s(sc):
            sc.wait_ge(s_x.h, X_LOADED)
        block.scalar(p1_pre_s)

        def p1_stats_v(vec):
            for i in range(NBLK):
                if i % 4 == 0:
                    vec.wait_ge(s_x.h, X_Q[i // 4])
                vec.reduce_sum(sp1[:, i:i + 1], x_tm[:, i * HID:(i + 1) * HID], axis=AX)
        block.vector(p1_stats_v)

        def p1_stats_s(sc):
            last = None
            for i in range(NBLK):
                if i % 4 == 0:
                    sc.wait_ge(s_x.h, X_Q[i // 4])
                last = sc.activation(sq_scr[:, 0:1024], x_tm[:, i * HID:(i + 1) * HID],
                                     ACT.Square, bias=0.0, scale=1.0,
                                     accum_out=sp1[:, 16 + i:17 + i])
            c_se.inc(last)
        block.scalar(p1_stats_s)
        P1_SS = c_se.n

        def p1_chain_v(vec):
            vec.wait_ge(c_se.h, P1_SS)
            vec.drain()
            vec.tensor_scalar(sp1[:, 32:48], sp1[:, 16:32], 1.0 / HID, None, ALU.mult)
            vec.tensor_scalar(sp1[:, 0:16], sp1[:, 0:16], 1.0 / HID, None, ALU.mult)
            vec.drain()
            vec.tensor_tensor(sp1[:, 48:64], sp1[:, 0:16], sp1[:, 0:16], ALU.mult)
            vec.drain()
            c_ve.inc(vec.scalar_tensor_tensor(sp1[:, 48:64], sp1[:, 32:48], EPS,
                                              sp1[:, 48:64], ALU.add, ALU.subtract))
        block.vector(p1_chain_v)
        P1_VAR = c_ve.n

        def p1_chain_s(sc):
            sc.wait_ge(c_ve.h, P1_VAR)
            c_se.inc(sc.activation(sp1[:, 32:48], sp1[:, 48:64], ACT.Sqrt,
                                   bias=0.0, scale=1.0))
        block.scalar(p1_chain_s)
        P1_SA = c_se.n

        def p1_negms_v(vec):
            vec.wait_ge(c_se.h, P1_SA)
            vec.reciprocal(sp1[:, 64:80], sp1[:, 32:48])
            vec.drain()
            c_ve.inc(vec.scalar_tensor_tensor(sp1[:, 80:96], sp1[:, 0:16], -1.0,
                                              sp1[:, 64:80], ALU.mult, ALU.mult))
        block.vector(p1_negms_v)
        P1_NEG = c_ve.n

        te_kv = [0] * NBLK
        se_kevac = [0] * NBLK
        ve_vevac = [0] * NBLK
        ve_lnfm = [0] * NBLK
        lnT_done_list = [0] * NBLK
        dsc = DeferredScalar()
        for i in range(NBLK):
            xblk = x_tm[:, i * HID:(i + 1) * HID]

            def v_apply(vec, i=i, xblk=xblk, pT=(lnT_done_list[i - 2] if i >= 2 else 0)):
                if pT:
                    vec.wait_ge(c_te.h, pT)
                c_ve.inc(vec.tensor_scalar(ln1tm(i)[:, :], xblk,
                                           sp1[:, 64 + i:65 + i],
                                           sp1[:, 80 + i:81 + i], ALU.mult, ALU.add))
            block.vector(v_apply)
            apply_done = c_ve.n

            def t_trans(t, i=i, ad=apply_done, prev_ev=(ve_lnfm[i - 1] if i > 0 else 0)):
                t.wait_ge(c_ve.h, ad)
                if i == 0:
                    t.wait_ge(s_wl.h, P1W_LOADED)
                if i > 0:
                    t.wait_ge(c_ve.h, prev_ev)
                dst = pp[7].ap().bitcast(BF16)
                last = None
                for c in range(NCH):
                    last = t.transpose(dst[:, c * 128:(c + 1) * 128],
                                       ln1tm(i)[:, c * 128:(c + 1) * 128], idn_bf[:])
                c_te.inc(last)
            block.tensor(t_trans)
            lnT_done = c_te.n
            lnT_done_list[i] = c_te.n

            def v_evac_ln(vec, i=i, td=lnT_done):
                vec.wait_ge(c_te.h, td)
                src = pp[7].ap().bitcast(BF16)
                c_ve.inc(vec.tensor_copy(
                    lnc_view[:, :, i * 128:(i + 1) * 128],
                    src[:].rearrange("p (c n) -> p c n", c=NCH)))
            block.vector(v_evac_ln)
            ve_lnfm[i] = c_ve.n

            def t_kv_proj(t, i=i, lev=ve_lnfm[i]):
                t.wait_ge(c_ve.h, lev)
                if i >= 1:
                    t.wait_ge(c_se.h, se_kevac[i - 1])
                    t.wait_ge(c_ve.h, ve_vevac[i - 1])
                last = None
                for c in range(NCH):
                    lhs = lnc_view[:, c, i * 128:(i + 1) * 128]
                    t.matmul(pp[3][:, :], lhs, wk_bf(c)[:, 0:512],
                             start=(c == 0), stop=(c == NCH - 1))
                    t.matmul(pp[4][:, :], lhs, wk_bf(c)[:, 512:1024],
                             start=(c == 0), stop=(c == NCH - 1))
                    t.matmul(pp[5][:, :], lhs, wv_bf(c)[:, 0:512],
                             start=(c == 0), stop=(c == NCH - 1))
                    last = t.matmul(pp[6][:, :], lhs, wv_bf(c)[:, 512:1024],
                                    start=(c == 0), stop=(c == NCH - 1))
                c_te.inc(last)
            block.tensor(t_kv_proj)
            kvproj_done = c_te.n

            se_kevac[i] = c_se.reserve(1)

            def s_kevac(sc, i=i, kd=kvproj_done, tk=(te_kv[i - 2] if i >= 2 else 0)):
                sc.wait_ge(c_te.h, kd)
                if tk:
                    sc.wait_ge(c_te.h, tk)
                sc.activation(ktm(i)[:, 0:512], pp[3][:, :], ACT.Sigmoid, bias=0.0, scale=1.0)
                sc.activation(ktm(i)[:, 512:1024], pp[4][:, :], ACT.Sigmoid,
                              bias=0.0, scale=1.0).then_inc(c_se.h, 1)
            dsc.add(s_kevac)

            def v_vevac(vec, i=i, kd=kvproj_done):
                vec.wait_ge(c_te.h, kd)
                if i >= 2:
                    vec.wait_ge(c_te.h, te_kv[i - 2])
                vec.tensor_copy(vtm(i)[:, 0:512], pp[5][:, :])
                c_ve.inc(vec.tensor_copy(vtm(i)[:, 512:1024], pp[6][:, :]))
            block.vector(v_vevac)
            ve_vevac[i] = c_ve.n

            def t_kv(t, i=i, ke=se_kevac[i], ve=ve_vevac[i]):
                t.wait_ge(c_se.h, ke)
                t.wait_ge(c_ve.h, ve)
                last = None
                for h in range(HEADS):
                    lhs = ktm(i)[:, h * DH:(h + 1) * DH]
                    bank = pp[0] if h < 8 else pp[1]
                    col = (h % 8) * DH
                    t.matmul(bank[0:DH, col:col + DH], lhs, vtm(i)[:, h * DH:(h + 1) * DH],
                             start=(i == 0 and h % 8 == 0), stop=(i == NBLK - 1),
                             skip_group_check=True)
                    last = t.matmul(pp[2][0:DH, h:h + 1], lhs, ones_bf[:],
                                    start=(i == 0 and h == 0), stop=(i == NBLK - 1),
                                    skip_group_check=True)
                c_te.inc(last)
            block.tensor(t_kv)
            te_kv[i] = c_te.n

        if DEBUG:
            def dbg_ln1(sync):
                sync.wait_ge(c_ve.h, ve_lnfm[NBLK - 1])
                s_dbg.inc(sync.dma_start(out=dbg["d_ln1"].ap(), in_=lnx_fm[:]), 16)
            block.sync(dbg_ln1)
        DBG_LN1 = s_dbg.n

        dsc.emit(block)

        def v_kv_evac(vec):
            vec.wait_ge(c_te.h, te_kv[NBLK - 1])
            kvr = kvf32.rearrange("p (h j) -> p h j", h=HEADS)
            vec.tensor_copy(kvr[:, 0:8, 0:DH],
                            pp[0][0:DH, :].rearrange("p (h j) -> p h j", h=8))
            vec.tensor_copy(kvr[:, 8:16, 0:DH],
                            pp[1][0:DH, :].rearrange("p (h j) -> p h j", h=8))
            c_ve.inc(vec.tensor_copy(kvr[:, :, DH:DH + 1],
                                     pp[2][0:DH, 0:HEADS].unsqueeze(2)))
        block.vector(v_kv_evac)
        KV_EVAC = c_ve.n
        if DEBUG:
            def dbg_kv(sync):
                sync.wait_ge(c_ve.h, KV_EVAC)
                s_dbg.inc(sync.dma_start(out=dbg["d_kv"].ap(), in_=kvf32), 16)
            block.sync(dbg_kv)

        def sc_ccin(sc):
            sc.wait_ge(c_ve.h, KV_EVAC)
            s_sq.inc(sc.dma_start(out=cc_in.ap(), in_=kvf32), 16)
        block.scalar(sc_ccin)
        CCIN = s_sq.n

        def gp_wq(gp):
            s_wl.inc(gp.dma_start(
                out=ap_bf16(A_WQ, 128, NCH * 1024).rearrange("p (c m) -> p c m", c=NCH),
                in_=wq_d.ap().rearrange("(c k) m -> k c m", c=NCH)), 16)
        block.gpsimd(gp_wq)
        WQ_LOADED = s_wl.n

        def gp_cc(gp):
            gp.wait_ge(s_sq.h, CCIN)
            s_cc.inc(gp.collective_compute(
                "AllReduce", ALU.add,
                replica_groups=[list(range(NCORES))],
                ins=[cc_in.ap().opt()],
                outs=[cc_out.ap().opt()]))
            gp.wait_ge(s_cc.h, s_cc.n)
            s_wl.inc(gp.dma_start(out=kvbf[0:64, :], in_=cc_out.ap()), 16)
            s_wl.inc(gp.dma_start(out=kvbf[64:128, :], in_=cc_out.ap()), 16)
            s_wl.inc(gp.dma_start(
                out=ap_bf16(A_WO, 128, NCH * 1024).rearrange("p (c m) -> p c m", c=NCH),
                in_=wo_d.ap().rearrange("(c k) m -> k c m", c=NCH)), 16)
        block.gpsimd(gp_cc)
        KV_READY = s_wl.n - 16
        if DEBUG:
            def dbg_kvbf(gp, kr=s_wl.n - 16):
                gp.wait_ge(s_wl.h, kr)
                s_dbg.inc(gp.dma_start(out=dbg["d_kvbf"].ap(), in_=kvbf[:, :]), 16)
            block.gpsimd(dbg_kvbf)
        WO_LOADED = s_wl.n

        se_qevac = [0] * NCH
        for m in range(NCH):
            def t_q(t, m=m):
                if m == 0:
                    t.wait_ge(s_wl.h, WQ_LOADED)
                    t.wait_ge(c_ve.h, KV_EVAC)
                    t.wait_ge(c_se.h, se_kevac[NBLK - 1])
                if m >= 2:
                    t.wait_ge(c_se.h, se_qevac[m - 2])
                banks = pp[0:4] if m % 2 == 0 else pp[4:8]
                last = None
                for c in range(NCH):
                    lhsT = wq_bf(c)[:, m * 128:(m + 1) * 128]
                    for seg in range(4):
                        last = t.matmul(banks[seg][:, :], lhsT,
                                        lnc_view[:, c, seg * 512:(seg + 1) * 512],
                                        start=(c == 0), stop=(c == NCH - 1))
                c_te.inc(last)
            block.tensor(t_q)
            q_done_m = c_te.n

            se_qevac[m] = c_se.reserve(1)

            def s_qevac(sc, m=m, qd=q_done_m):
                sc.wait_ge(c_te.h, qd)
                banks = pp[0:4] if m % 2 == 0 else pp[4:8]
                last = None
                for seg in range(4):
                    last = sc.activation(qbf()[:, m * T + seg * 512:m * T + (seg + 1) * 512],
                                         banks[seg][:, :], ACT.Sigmoid, bias=0.0, scale=1.0)
                last.then_inc(c_se.h, 1)
            dsc.add(s_qevac)
            if DEBUG and m == 0:
                def dbg_q2(sync, qe=c_se.n):
                    sync.wait_ge(c_se.h, qe)
                    s_dbg.inc(sync.dma_start(out=dbg["d_q2"].ap(), in_=qbf()[:, 0:2048]), 16)
                    s_dbg.inc(sync.dma_start(out=dbg["d_wq0"].ap(), in_=wq_bf(0)), 16)
                block.sync(dbg_q2)
        Q_DONE = se_qevac[NCH - 1]
        if DEBUG:
            def dbg_q(sync):
                sync.wait_ge(c_se.h, Q_DONE)
                s_dbg.inc(sync.dma_start(out=dbg["d_q"].ap(), in_=qbf()), 16)
            block.sync(dbg_q)
        DBG_Q = s_dbg.n

        NSTRIP = 4
        ve_r2 = [0] * NBLK
        ve_ln2fm = [0] * NBLK
        prev_h_evac = 0
        prev_num_evac = {}
        prev_T_evac = {}
        prev_gTe = {}
        prev_lnTe = {}
        strip_T_done = [0] * NSTRIP
        ao1_done_l = {}
        gated_last = [0]
        dsc2 = DeferredScalar()

        def emit_num_phase(s, inject=None):
            tgt = dsc if inject is not None else dsc2

            def emit_T(h):
                def t_numT(t, s=s, h=h, nev=prev_num_evac[(s, h)],
                           pg=(prev_gTe.get((s - 1, 2 + h % 2), 0) if h < 2 else 0)):
                    t.wait_ge(c_se.h, nev)
                    key = (s, h - 2) if h >= 2 else (s - 1, h + HEADS - 2)
                    if key in prev_T_evac:
                        t.wait_ge(c_ve.h, prev_T_evac[key])
                    if pg:
                        t.wait_ge(c_ve.h, pg)
                    bank = pp[6 + (h % 2)]
                    last = None
                    for b4 in range(4):
                        last = t.transpose(bank[:, b4 * 65:b4 * 65 + 65],
                                           numsl(h)[0:65, b4 * 128:(b4 + 1) * 128],
                                           idn[0:65, 0:65])
                    c_te.inc(last)
                block.tensor(t_numT)
                numT_done = c_te.n

                def v_tev(vec, h=h, td=numT_done):
                    vec.wait_ge(c_te.h, td)
                    bank = pp[6 + (h % 2)]
                    b3 = bank[:, 0:260].rearrange("p (b j) -> p b j", b=4)
                    vec.reciprocal(rc_sb[:, h * 4:h * 4 + 4], b3[:, :, 64:65].opt())
                    vec.drain()
                    last = None
                    for b4 in range(4):
                        last = vec.tensor_scalar(
                            atm()[:, b4 * 1040 + h * 65:b4 * 1040 + (h + 1) * 65],
                            bank[:, b4 * 65:(b4 + 1) * 65],
                            rc_sb[:, h * 4 + b4:h * 4 + b4 + 1], None, ALU.mult)
                    c_ve.inc(last)
                block.vector(v_tev)
                prev_T_evac[(s, h)] = c_ve.n

            for h in range(HEADS):
                def t_num(t, s=s, h=h):
                    if s == 0 and h == 0:
                        t.wait_ge(s_wl.h, KV_READY + 16)
                        t.wait_ge(c_se.h, Q_DONE)
                    key = (s, h - 2) if h >= 2 else (s - 1, h + HEADS - 2)
                    if key in prev_num_evac:
                        t.wait_ge(c_se.h, prev_num_evac[key])
                    base = (h % 2) * 64
                    bank = pp[4 + (h % 2)]
                    c_te.inc(t.matmul(
                        bank[0:65, :],
                        kvbf[base:base + 64, h * 65:(h + 1) * 65],
                        qbf()[base:base + 64,
                              (h // 2) * T + s * 512:(h // 2) * T + (s + 1) * 512],
                        start=True, stop=True))
                block.tensor(t_num)
                num_done = c_te.n
                prev_num_evac[(s, h)] = c_se.reserve(1)

                def s_nev(sc, s=s, h=h, nd=num_done,
                          gw=(gated_last[0] if (s > 0 and h == 0) else 0)):
                    sc.wait_ge(c_te.h, nd)
                    if gw:
                        sc.wait_ge(c_ve.h, gw)
                    sc.copy(numsl(h)[0:65, :], pp[4 + (h % 2)][0:65, :]).then_inc(c_se.h, 1)
                tgt.add(s_nev)
                if h >= 1:
                    emit_T(h - 1)
                if inject and h in inject:
                    for fn in inject[h]:
                        fn()
            emit_T(HEADS - 1)
            strip_T_done[s] = prev_T_evac[(s, HEADS - 1)]

        emit_num_phase(0)
        for s in range(NSTRIP):
            blks = [s * 4 + j for j in range(4)]

            vh_l = {}
            lss_box = [0]

            def mk_t_h(bi):
                b = blks[bi]

                def t_h(t, s=s, b=b, bi=bi,
                        pl=(prev_lnTe.get((s - 1, 3), 0) if (bi == 0 and s > 0) else 0),
                        pa=(ao1_done_l.get((s - 1, 3), 0) if (bi == 1 and s > 0) else 0),
                        vh=vh_l.get(bi - 2, 0)):
                    if s == 0 and bi == 0:
                        t.wait_ge(c_se.h, Q_DONE)
                        t.wait_ge(c_ve.h, CB_DONE)
                    if pl:
                        t.wait_ge(c_ve.h, pl)
                    if pa:
                        t.wait_ge(c_ve.h, pa)
                    if vh:
                        t.wait_ge(c_ve.h, vh)
                    p0, p1 = (0, 1) if bi % 2 == 0 else (2, 3)
                    last = None
                    for c in range(NCH):
                        lhs = lnc_view[:, c, b * 128:(b + 1) * 128]
                        t.matmul(pp[p0][:, :], lhs, wh_bf(c)[:, 0:512],
                                 start=(c == 0), stop=(c == NCH - 1))
                        last = t.matmul(pp[p1][:, :], lhs, wh_bf(c)[:, 512:1024],
                                        start=(c == 0), stop=(c == NCH - 1))
                    c_te.inc(last)
                block.tensor(t_h)
                return c_te.n

            def mk_v_h(bi, hd):
                b = blks[bi]

                def v_h(vec, b=b, bi=bi, hd=hd):
                    p0, p1 = (0, 1) if bi % 2 == 0 else (2, 3)
                    vec.wait_ge(c_te.h, hd)
                    vec.tensor_tensor(hpc(b)[:, 0:512], pp[p0][:, :], cb_b[:, 0:512], ALU.add)
                    c_ve.inc(vec.tensor_tensor(hpc(b)[:, 512:1024], pp[p1][:, :],
                                               cb_b[:, 512:1024], ALU.add))
                block.vector(v_h)
                vh_l[bi] = c_ve.n
                return c_ve.n

            th0 = mk_t_h(0)
            th1 = mk_t_h(1)

            def vA(vec, s=s):
                vec.drain()
                for bi in range(4):
                    at3 = atm()[:, bi * 1040:bi * 1040 + 1040].rearrange(
                        "p (h j) -> p h j", h=HEADS)
                    vec.reduce_sum(sp1[:, bi:bi + 1], at3[:, :, 0:64], axis=AXY)
                c_ve.inc(vec.drain())
            block.vector(vA)
            DIVRED = c_ve.n
            mk_v_h(0, th0)
            mk_v_h(1, th1)
            th2 = mk_t_h(2)
            th3 = mk_t_h(3)
            for f in dsc2.fns:
                dsc.add(f)
            dsc2.fns = []
            ASS_DONE = c_se.reserve(1)

            def sA(sc, s=s, dr=DIVRED):
                sc.wait_ge(c_ve.h, dr)
                last = None
                for bi in range(4):
                    at_str = atm()[:, bi * 1040:bi * 1040 + 1040].rearrange(
                        "p (h j) -> p h j", h=HEADS)[:, :, 0:64]
                    last = sc.activation(
                        gfm()[:, 0:1024].rearrange("p (h j) -> p h j", h=HEADS),
                        at_str, ACT.Square, bias=0.0, scale=1.0,
                        accum_out=sp1[:, 4 + bi:5 + bi])
                last.then_inc(c_se.h, 1)
            dsc.add(sA)

            def vB(vec):
                vec.wait_ge(c_se.h, ASS_DONE)
                vec.drain()
                vec.tensor_scalar(sp1[:, 8:12], sp1[:, 4:8], 1.0 / HID, None, ALU.mult)
                vec.tensor_scalar(sp1[:, 0:4], sp1[:, 0:4], 1.0 / HID, None, ALU.mult)
                vec.drain()
                vec.tensor_tensor(sp1[:, 12:16], sp1[:, 0:4], sp1[:, 0:4], ALU.mult)
                vec.drain()
                c_ve.inc(vec.scalar_tensor_tensor(sp1[:, 8:12], sp1[:, 8:12], EPS,
                                                  sp1[:, 12:16], ALU.add, ALU.subtract))
            block.vector(vB)
            AVAR_DONE = c_ve.n
            mk_v_h(2, th2)
            ASD_DONE = c_se.reserve(1)

            def sB(sc, av=AVAR_DONE):
                sc.wait_ge(c_ve.h, av)
                sc.activation(sp1[:, 12:16], sp1[:, 8:12], ACT.Sqrt,
                              bias=0.0, scale=1.0).then_inc(c_se.h, 1)
            dsc.add(sB)

            def vC(vec):
                vec.wait_ge(c_se.h, ASD_DONE)
                vec.reciprocal(sp1[:, 16:20], sp1[:, 12:16])
                vec.drain()
                c_ve.inc(vec.scalar_tensor_tensor(sp1[:, 20:24], sp1[:, 0:4], -1.0,
                                                  sp1[:, 16:20], ALU.mult, ALU.mult))
            block.vector(vC)
            ASTATS = c_ve.n
            mk_v_h(3, th3)
            prev_h_evac = vh_l[3]

            gT_done_l = {}
            wo_done_l = {}

            def mk_ao1(bi):
                boff = bi * 1040

                def v_ao1(vec, bi=bi, wd=wo_done_l[bi], boff=boff):
                    vec.wait_ge(c_te.h, wd)
                    t1 = atm()[:, boff:boff + 1024]
                    vec.tensor_tensor(t1[:, 0:512], pp[2][:, :], bo_b[:, 0:512], ALU.add)
                    c_ve.inc(vec.tensor_tensor(t1[:, 512:1024], pp[3][:, :],
                                               bo_b[:, 512:1024], ALU.add))
                block.vector(v_ao1)
                ao1_done_l[(s, bi)] = c_ve.n

            def mk_r2m(bi):
                b = blks[bi]
                boff = bi * 1040

                def v_r2(vec, b=b, boff=boff):
                    vec.drain()
                    xb = x_tm[:, b * HID:(b + 1) * HID]
                    c_ve.inc(vec.scalar_tensor_tensor(
                        xb, atm()[:, boff:boff + 1024], 0.0, xb, ALU.max, ALU.add))
                block.vector(v_r2)
                ve_r2[b] = c_ve.n

                def v_mean(vec, b=b, bi=bi):
                    vec.drain()
                    c_ve.inc(vec.reduce_sum(sp1[:, 32 + bi:33 + bi],
                                            x_tm[:, b * HID:(b + 1) * HID], axis=AX))
                block.vector(v_mean)
                if bi == 3:
                    lss_box[0] = c_se.reserve(1)

                def sC_b(sc, b=b, bi=bi, rw=ve_r2[b], inc=(bi == 3)):
                    sc.wait_ge(c_ve.h, rw)
                    a = sc.activation(atm()[:, bi * 1040:bi * 1040 + 1024],
                                      x_tm[:, b * HID:(b + 1) * HID],
                                      ACT.Square, bias=0.0, scale=1.0,
                                      accum_out=sp1[:, 36 + bi:37 + bi])
                    if inc:
                        a.then_inc(c_se.h, 1)
                dsc.add(sC_b)

            for bi, b in enumerate(blks):
                boff = bi * 1040
                at_str = atm()[:, boff:boff + 1040].rearrange(
                    "p (h j) -> p h j", h=HEADS)[:, :, 0:64]

                def v_apply_gate(vec, s=s, b=b, bi=bi, at_str=at_str,
                                 gtp=gT_done_l.get(bi - 2, 0)):
                    if gtp:
                        vec.wait_ge(c_te.h, gtp)
                    vec.tensor_scalar(at_str, at_str, sp1[:, 16 + bi:17 + bi],
                                      sp1[:, 20 + bi:21 + bi], ALU.mult, ALU.add)
                    vec.drain()
                    c_ve.inc(vec.tensor_tensor(
                        gtm(bi)[:, :].rearrange("p (h j) -> p h j", h=HEADS),
                        at_str,
                        hpc(b)[:].rearrange("p (h j) -> p h j", h=HEADS),
                        ALU.mult))
                block.vector(v_apply_gate)
                gated_done = c_ve.n
                if bi == 3:
                    gated_last[0] = c_ve.n

                def t_gT(t, s=s, bi=bi, gd=gated_done,
                         ptv=prev_T_evac.get((s, 14 + bi % 2), 0),
                         pge=prev_gTe.get((s, bi - 2), 0)):
                    t.wait_ge(c_ve.h, gd)
                    if bi < 2 and ptv:
                        t.wait_ge(c_ve.h, ptv)
                    if pge:
                        t.wait_ge(c_ve.h, pge)
                    bank = pp[6 + bi % 2].ap().bitcast(BF16)
                    last = None
                    for c in range(NCH):
                        last = t.transpose(bank[:, c * 128:(c + 1) * 128],
                                           gtm(bi)[:, c * 128:(c + 1) * 128], idn_bf[:])
                    c_te.inc(last)
                block.tensor(t_gT)
                gT_done_l[bi] = c_te.n

                if bi > 0:
                    mk_ao1(bi - 1)

                def v_gTe(vec, bi=bi, gd=gT_done_l[bi]):
                    vec.wait_ge(c_te.h, gd)
                    src2 = pp[6 + bi % 2].ap().bitcast(BF16)
                    dst = gfm()[:].rearrange("p (c n) -> p c n", c=NCH)[
                        :, :, bi * 128:(bi + 1) * 128]
                    c_ve.inc(vec.tensor_copy(dst, src2[:].rearrange("p (c n) -> p c n", c=NCH)))
                block.vector(v_gTe)
                prev_gTe[(s, bi)] = c_ve.n
                gTe_done = c_ve.n

                def t_wo(t, s=s, bi=bi, ge=gTe_done, vh=vh_l[3],
                         pa=ao1_done_l.get((s, bi - 1)) or (ao1_done_l.get((s - 1, 3), 0) if bi == 0 else 0)):
                    t.wait_ge(c_ve.h, ge)
                    if s == 0 and bi == 0:
                        t.wait_ge(s_wl.h, WO_LOADED)
                        t.wait_ge(c_se.h, Q_DONE)
                    if bi == 0:
                        t.wait_ge(c_ve.h, vh)
                    if pa:
                        t.wait_ge(c_ve.h, pa)
                    gc = gfm()[:].rearrange("p (c n) -> p c n", c=NCH)
                    last = None
                    for c in range(NCH):
                        lhs = gc[:, c, bi * 128:(bi + 1) * 128]
                        t.matmul(pp[2][:, :], lhs, wo_bf(c)[:, 0:512],
                                 start=(c == 0), stop=(c == NCH - 1))
                        last = t.matmul(pp[3][:, :], lhs, wo_bf(c)[:, 512:1024],
                                        start=(c == 0), stop=(c == NCH - 1))
                    c_te.inc(last)
                block.tensor(t_wo)
                wo_done_l[bi] = c_te.n
                if bi > 0:
                    mk_r2m(bi - 1)
            mk_ao1(3)
            mk_r2m(3)

            def vE(vec, lss=lss_box[0]):
                vec.wait_ge(c_se.h, lss)
                vec.drain()
                vec.tensor_scalar(sp1[:, 40:44], sp1[:, 36:40], 1.0 / HID, None, ALU.mult)
                vec.tensor_scalar(sp1[:, 32:36], sp1[:, 32:36], 1.0 / HID, None, ALU.mult)
                vec.drain()
                vec.tensor_tensor(sp1[:, 44:48], sp1[:, 32:36], sp1[:, 32:36], ALU.mult)
                vec.drain()
                c_ve.inc(vec.scalar_tensor_tensor(sp1[:, 40:44], sp1[:, 40:44], EPS,
                                                  sp1[:, 44:48], ALU.add, ALU.subtract))
            block.vector(vE)
            LVAR_DONE = c_ve.n
            SD_TGT = s + 1

            def sD(sc, lv=LVAR_DONE):
                sc.wait_ge(c_ve.h, lv)
                sc.activation(sp1[:, 44:48], sp1[:, 40:44], ACT.Sqrt,
                              bias=0.0, scale=1.0).then_inc(s_sd.h, 1)

            def make_sD():
                s_sd.reserve(1)
                dsc.add(sD)

            def vF(vec, sdt=SD_TGT):
                vec.wait_ge(s_sd.h, sdt)
                vec.reciprocal(sp1[:, 48:52], sp1[:, 44:48])
                vec.drain()
                c_ve.inc(vec.scalar_tensor_tensor(sp1[:, 52:56], sp1[:, 32:36], -1.0,
                                                  sp1[:, 48:52], ALU.mult, ALU.mult))
            block.vector(vF)
            LSTATS = c_ve.n

            a2_l = {}
            lnT_l = {}

            def emit_a2(bi, s=s):
                b = blks[bi]

                def v_apply2(vec, b=b, bi=bi,
                             gate=(gT_done_l.get(bi + 2) if bi < 2 else lnT_l.get(bi - 2, 0))):
                    if gate:
                        vec.wait_ge(c_te.h, gate)
                    c_ve.inc(vec.tensor_scalar(gtm(bi)[:, :], x_tm[:, b * HID:(b + 1) * HID],
                                               sp1[:, 48 + bi:49 + bi],
                                               sp1[:, 52 + bi:53 + bi], ALU.mult, ALU.add))
                block.vector(v_apply2)
                a2_l[bi] = c_ve.n

            def emit_lnT(bi, s=s):
                def t_ln2T(t, s=s, bi=bi, ad=a2_l[bi], hv=prev_h_evac,
                           pv=prev_lnTe.get((s, bi - 1), 0)):
                    t.wait_ge(c_ve.h, ad)
                    if bi == 0:
                        t.wait_ge(c_ve.h, hv)
                    if pv:
                        t.wait_ge(c_ve.h, pv)
                    bank = pp[1].ap().bitcast(BF16)
                    last = None
                    for c in range(NCH):
                        last = t.transpose(bank[:, c * 128:(c + 1) * 128],
                                           gtm(bi)[:, c * 128:(c + 1) * 128], idn_bf[:])
                    c_te.inc(last)
                block.tensor(t_ln2T)
                lnT_l[bi] = c_te.n

            def emit_lne(bi, s=s):
                b = blks[bi]

                def v_ln2e(vec, b=b, bi=bi, td=lnT_l[bi]):
                    vec.wait_ge(c_te.h, td)
                    src2 = pp[1].ap().bitcast(BF16)
                    dst = lnc_view[:, :, b * 128:(b + 1) * 128]
                    c_ve.inc(vec.tensor_copy(dst, src2[:].rearrange("p (c n) -> p c n", c=NCH)))
                block.vector(v_ln2e)
                prev_lnTe[(s, bi)] = c_ve.n
                ve_ln2fm[b] = c_ve.n

            emit_a2(0)
            emit_a2(1)
            if s + 1 < NSTRIP:
                emit_num_phase(s + 1, inject={
                    1: [make_sD],
                    4: [lambda: emit_lnT(0), lambda: emit_a2(2)],
                    6: [lambda: emit_lne(0)],
                    8: [lambda: emit_lnT(1), lambda: emit_a2(3)],
                    10: [lambda: emit_lne(1)],
                    12: [lambda: emit_lnT(2)],
                    13: [lambda: emit_lne(2)],
                    14: [lambda: emit_lnT(3)]})
                emit_lne(3)
            else:
                make_sD()
                emit_lnT(0)
                emit_a2(2)
                emit_lne(0)
                emit_lnT(1)
                emit_a2(3)
                emit_lne(1)
                emit_lnT(2)
                emit_lne(2)
                emit_lnT(3)
                emit_lne(3)

        STRIPS_TE = c_te.n
        STRIPS_VE = c_ve.n
        STRIPS_SE = c_se.n

        prev_eT_te = 0
        prev_p7_ve = 0
        for s in range(NSTRIP):
            def t_rt(t, s=s, le=ve_ln2fm[NBLK - 1], ph=prev_h_evac,
                     st3=prev_num_evac[(NSTRIP - 1, HEADS - 1)]):
                if s == 0:
                    t.wait_ge(c_ve.h, le)
                    t.wait_ge(c_ve.h, ph)
                    t.wait_ge(c_se.h, st3)
                last = None
                for c in range(NCH):
                    last = t.matmul(pp[4][0:8, :], wrt_bf[:, c * 8:(c + 1) * 8],
                                    lnc_view[:, c, s * 512:(s + 1) * 512],
                                    start=(c == 0), stop=(c == NCH - 1))
                c_te.inc(last)
            block.tensor(t_rt)
            rt_done = c_te.n

            exp_done = c_se.reserve(1)

            def s_exp(sc, rd=rt_done, pe=prev_eT_te, lt=ve_ln2fm[NBLK - 1]):
                sc.wait_ge(c_te.h, rd)
                if pe:
                    sc.wait_ge(c_te.h, pe)
                else:
                    sc.wait_ge(c_ve.h, lt)
                sc.activation(e_sb()[:, :], pp[4][0:8, :], ACT.Exp,
                              bias=0.0, scale=1.0).then_inc(c_se.h, 1)
            dsc.add(s_exp)

            def t_eT(t, ed=exp_done, pv=prev_p7_ve):
                t.wait_ge(c_se.h, ed)
                if pv:
                    t.wait_ge(c_ve.h, pv)
                last = None
                for b4 in range(4):
                    last = t.transpose(pp[5][:, b4 * 8:b4 * 8 + 8],
                                       e_sb()[0:8, b4 * 128:(b4 + 1) * 128], idn[0:8, 0:8])
                c_te.inc(last)
            block.tensor(t_eT)
            eT_done = c_te.n
            prev_eT_te = c_te.n

            def v_p7(vec, s=s, ed=eT_done):
                vec.wait_ge(c_te.h, ed)
                vec.tensor_copy(eT_sb[:, 0:32], pp[5][:, 0:32])
                vec.drain()
                for b4 in range(4):
                    vec.reduce_sum(stat[:, S_T0:S_T0 + 1], eT_sb[:, b4 * 8:(b4 + 1) * 8],
                                   axis=AX)
                    vec.drain()
                    vec.reciprocal(stat[:, S_T1:S_T1 + 1], stat[:, S_T0:S_T0 + 1])
                    vec.drain()
                    vec.tensor_scalar(p7_tm[:, s * 4 + b4:s * 4 + b4 + 1],
                                      eT_sb[:, b4 * 8 + 7:b4 * 8 + 8],
                                      stat[:, S_T1:S_T1 + 1], None, ALU.mult)
                c_ve.inc(vec.drain())
            block.vector(v_p7)
            prev_p7_ve = c_ve.n
        P7_DONE = c_ve.n
        dsc.emit(block)
        if DEBUG:
            def dbg_r2(sync):
                sync.wait_ge(c_ve.h, ve_r2[NBLK - 1])
                s_dbg.inc(sync.dma_start(out=dbg["d_r2"].ap(), in_=x_tm[:]), 16)
                sync.wait_ge(c_ve.h, P7_DONE)
                s_dbg.inc(sync.dma_start(out=dbg["d_p7"].ap(), in_=p7_tm[:]), 16)
            block.sync(dbg_r2)
        DBG_END2A = s_dbg.n
        A2A_TE_DONE = c_te.n
        A2A_VE_DONE = c_ve.n
        A2A_SE_DONE = c_se.n

        NFS = 2
        te_w1_consumed = []
        te_w2_consumed = []
        se_gelu = {}
        ve_yev = {}
        ve_out = [0] * NBLK
        WL_BASE = s_wl.n

        def w1_count(S, m):
            return WL_BASE + 16 * (S * (NMCH + NCH) + m + 1)

        def w2_count(S, j):
            return WL_BASE + 16 * (S * (NMCH + NCH) + NMCH + j + 1)

        for S in range(NFS):
            for m in range(NMCH):
                def t_u(t, S=S, m=m):
                    if S == 0 and m == 0:
                        t.wait_ge(c_ve.h, STRIPS_VE)
                        t.wait_ge(c_se.h, STRIPS_SE)
                    if S > 0 and m == 0:
                        t.wait_ge(c_ve.h, ve_out[S * 8 - 1])
                    t.wait_ge(s_wl.h, w1_count(S, m))
                    key = m - 2
                    if key >= 0:
                        t.wait_ge(c_se.h, se_gelu[(S, key)])
                    elif S > 0:
                        t.wait_ge(c_se.h, se_gelu[(S - 1, key + NMCH)])
                    pk = (pp[0], pp[1]) if m % 2 == 0 else (pp[2], pp[3])
                    last = None
                    for c in range(NCH):
                        lhsT = w1r(m)[:, c * 128:(c + 1) * 128]
                        t.matmul(pk[0][:, :], lhsT,
                                 lnc_view[:, c, S * 1024:S * 1024 + 512],
                                 start=(c == 0), stop=(c == NCH - 1))
                        last = t.matmul(pk[1][:, :], lhsT,
                                        lnc_view[:, c, S * 1024 + 512:S * 1024 + 1024],
                                        start=(c == 0), stop=(c == NCH - 1))
                    c_te.inc(last)
                block.tensor(t_u)
                te_w1_consumed.append(c_te.n)
                u_done = c_te.n
                se_gelu[(S, m)] = c_se.reserve(1)

                def s_gelu(sc, m=m, ud=u_done, dq=(DBG_END2A if DEBUG else 0), S=S):
                    sc.wait_ge(c_te.h, ud)
                    if DEBUG and m == 0 and S == 0:
                        sc.wait_ge(s_dbg.h, dq)
                    pk = (pp[0], pp[1]) if m % 2 == 0 else (pp[2], pp[3])
                    sc.activation(ug(m)[:, 0:512], pk[0][:, :], ACT.Gelu, bias=0.0, scale=1.0)
                    sc.activation(ug(m)[:, 512:1024], pk[1][:, :], ACT.Gelu,
                                  bias=0.0, scale=1.0).then_inc(c_se.h, 1)
                dsc.add(s_gelu)
            GELU_ALL = se_gelu[(S, NMCH - 1)]

            for j in range(NCH):
                def t_y(t, S=S, j=j, ga=GELU_ALL):
                    t.wait_ge(s_wl.h, w2_count(S, j))
                    t.wait_ge(c_se.h, ga)
                    if S == 0 and j == 0:
                        t.wait_ge(c_ve.h, P7_DONE)
                    key = j - 2
                    if key >= 0:
                        t.wait_ge(c_ve.h, ve_yev[(S, key)])
                    elif S > 0:
                        t.wait_ge(c_ve.h, ve_yev[(S - 1, key + NCH)])
                    pk = (pp[4], pp[5]) if j % 2 == 0 else (pp[6], pp[7])
                    last = None
                    for m in range(NMCH):
                        lhsT = w2r(j)[:, m * 128:(m + 1) * 128]
                        t.matmul(pk[0][:, :], lhsT, ug(m)[:, 0:512],
                                 start=(m == 0), stop=(m == NMCH - 1))
                        last = t.matmul(pk[1][:, :], lhsT, ug(m)[:, 512:1024],
                                        start=(m == 0), stop=(m == NMCH - 1))
                    c_te.inc(last)
                block.tensor(t_y)
                te_w2_consumed.append(c_te.n)
                y_done = c_te.n

                def v_yev(vec, j=j, yd=y_done):
                    vec.wait_ge(c_te.h, yd)
                    pk = (pp[4], pp[5]) if j % 2 == 0 else (pp[6], pp[7])
                    vec.tensor_copy(yfm(j)[:, 0:512], pk[0][:, :])
                    c_ve.inc(vec.tensor_copy(yfm(j)[:, 512:1024], pk[1][:, :]))
                block.vector(v_yev)
                ve_yev[(S, j)] = c_ve.n
            Y_ALL = ve_yev[(S, NCH - 1)]
            if DEBUG and S == 0:
                def dbg_y(sync, ya=Y_ALL):
                    sync.wait_ge(c_ve.h, ya)
                    s_dbg.inc(sync.dma_start(out=dbg["d_y"].ap(),
                                             in_=ap_bf16(A_YFM, 128, NCH * 1024)), 16)
                block.sync(dbg_y)
            DBG_Y = s_dbg.n

            for bi in range(8):
                gblk = S * 8 + bi

                def t_yT(t, S=S, bi=bi, ya=Y_ALL, gblk=gblk, ga=GELU_ALL):
                    t.wait_ge(c_ve.h, ya)
                    t.wait_ge(c_se.h, ga)
                    if gblk >= 2:
                        t.wait_ge(c_ve.h, ve_out[gblk - 2])
                    bank = pp[gblk % 2].ap().bitcast(BF16)
                    last = None
                    for j in range(NCH):
                        last = t.transpose(bank[:, j * 128:(j + 1) * 128],
                                           yfm(j)[:, bi * 128:(bi + 1) * 128], idn_bf[:])
                    c_te.inc(last)
                block.tensor(t_yT)
                yT_done = c_te.n

                def v_out(vec, gblk=gblk, yd=yT_done, dgy=(DBG_Y if DEBUG else 0)):
                    vec.wait_ge(c_te.h, yd)
                    vec.wait_ge(c_ve.h, P7_DONE)
                    if DEBUG:
                        vec.wait_ge(s_dbg.h, dgy)
                    bank = pp[gblk % 2].ap().bitcast(BF16)
                    xb = x_tm[:, gblk * HID:(gblk + 1) * HID]
                    c_ve.inc(vec.scalar_tensor_tensor(
                        xb, bank[:, 0:1024], p7_tm[:, gblk:gblk + 1], xb, ALU.mult, ALU.add))
                block.vector(v_out)
                ve_out[gblk] = c_ve.n

            def sy_store(sync, S=S, vo1=ve_out[S * 8 + 3], vo2=ve_out[S * 8 + 7]):
                orr = out_d.ap().rearrange("(t p) k -> p t k", p=128)
                xrr = x_tm[:].rearrange("p (t k) -> p t k", t=NBLK)
                sync.wait_ge(c_ve.h, vo1)
                s_st.inc(sync.dma_start(out=orr[:, S * 8:S * 8 + 4, :],
                                        in_=xrr[:, S * 8:S * 8 + 4, :]), 16)
                sync.wait_ge(c_ve.h, vo2)
                s_st.inc(sync.dma_start(out=orr[:, S * 8 + 4:S * 8 + 8, :],
                                        in_=xrr[:, S * 8 + 4:S * 8 + 8, :]), 16)
            block.sync(sy_store)

        dsc.emit(block)

        def gp_ffn(gp):
            gp.wait_ge(c_te.h, STRIPS_TE)
            gp.wait_ge(c_ve.h, STRIPS_VE)
            gp.wait_ge(c_se.h, STRIPS_SE)
            n = 0
            for S in range(NFS):
                for m in range(NMCH):
                    idx = S * NMCH + m
                    if idx >= 4:
                        gp.wait_ge(c_te.h, te_w1_consumed[idx - 4])
                    gp.dma_start(
                        out=w1r(m).rearrange("p (c j) -> p c j", c=NCH),
                        in_=w1_d.ap()[:, m * 128:(m + 1) * 128].rearrange(
                            "(c k) j -> k c j", c=NCH)).then_inc(s_wl.h, 16)
                    n += 16
                for j in range(NCH):
                    idx = S * NCH + j
                    if idx >= 2:
                        gp.wait_ge(c_te.h, te_w2_consumed[idx - 2])
                    gp.dma_start(
                        out=w2r(j).rearrange("p (mc jj) -> p mc jj", mc=NMCH),
                        in_=w2_d.ap()[:, j * 128:(j + 1) * 128].rearrange(
                            "(mc k) jj -> k mc jj", mc=NMCH)).then_inc(s_wl.h, 16)
                    n += 16
            s_wl.n += 0
        block.gpsimd(gp_ffn)

        def sy_final(sync):
            sync.wait_ge(s_st.h, s_st.n)
        block.sync(sy_final)

    return nc


def _permutation():
    perm = np.empty(HID, dtype=np.int64)
    for fp in range(HID):
        h, d = fp // DH, fp % DH
        perm[fp] = d * HEADS + h
    return perm


def kernel(L0_attn_x, L0_attn_Wh, L0_attn_bh, L0_attn_Wq, L0_attn_bq, L0_attn_Wk,
           L0_attn_bk, L0_attn_Wv, L0_attn_bv, L0_attn_beta, L0_attn_ln_w,
           L0_attn_ln_b, L0_attn_Wo, L0_attn_bo, L0_ffn_Wr, L0_ffn_W1, L0_ffn_b1,
           L0_ffn_W2, L0_ffn_b2):
    perm = _permutation()
    x = np.ascontiguousarray(np.asarray(L0_attn_x, dtype=np.float32).reshape(SEQ, HID))
    wq = np.ascontiguousarray(np.asarray(L0_attn_Wq, np.float32)[:, perm])
    wk = np.ascontiguousarray(np.asarray(L0_attn_Wk, np.float32)[:, perm])
    wv = np.ascontiguousarray(np.asarray(L0_attn_Wv, np.float32)[:, perm])
    wh = np.ascontiguousarray(np.asarray(L0_attn_Wh, np.float32)[:, perm])
    wo = np.ascontiguousarray(np.asarray(L0_attn_Wo, np.float32)[perm, :])
    bh = np.ascontiguousarray(np.asarray(L0_attn_bh, np.float32)[perm].reshape(1, HID))
    beta = np.ascontiguousarray(np.asarray(L0_attn_beta, np.float32)[perm].reshape(1, HID))
    bo = np.ascontiguousarray(np.asarray(L0_attn_bo, np.float32).reshape(1, HID))
    wrt = np.ascontiguousarray(np.asarray(L0_ffn_Wr, np.float32).T)
    w1 = np.ascontiguousarray(np.asarray(L0_ffn_W1, np.float32)[7])
    w2 = np.ascontiguousarray(np.asarray(L0_ffn_W2, np.float32)[7])
    idn = np.eye(128, dtype=np.float32)

    nc = build_nc()
    in_maps = []
    for c in range(NCORES):
        in_maps.append({
            "x": np.ascontiguousarray(x[c * T:(c + 1) * T]),
            "wq": wq, "wk": wk, "wv": wv, "wh": wh, "wo": wo,
            "bh": bh, "beta": beta, "bo": bo, "wrt": wrt,
            "w1": w1, "w2": w2, "idn_h": idn,
        })
    res = run_bass_kernel_spmd(nc, in_maps, core_ids=list(range(NCORES)))
    if DEBUG:
        global DBG_RESULTS
        DBG_RESULTS = res.results
    out = np.concatenate([res.results[c]["out"] for c in range(NCORES)], axis=0)
    return out.reshape(1, SEQ, HID).astype(np.float32)


# version 22
# speedup vs baseline: 1.3621x; 1.0228x over previous
import contextlib
import numpy as np
import concourse.bass as bass
import concourse.mybir as mybir
from concourse.bass_utils import run_bass_kernel_spmd

F32 = mybir.dt.float32
BF16 = mybir.dt.bfloat16
AX = mybir.AxisListType.X
AXY = mybir.AxisListType.XY
ALU = mybir.AluOpType
ACT = mybir.ActivationFunctionType

NCORES = 8
HID = 1024
HEADS = 16
DH = 64
SEQ = 16384
T = SEQ // NCORES
NBLK = T // 128
NCH = HID // 128
EPS = 1e-5
FF = 4096
NMCH = FF // 128

A_QBF = 0
A_WK = 0
A_WV = 16384
A_KTM = 32768
A_HPC = 32768
A_VTM = 36864
A_ATM = 36864
A_LN1TM = 40960
A_KVF = 40960
A_WQ = 45184
A_GFM = 53504
A_NUM = 61696
A_CBB = 65792
A_BOB = 67840
A_KVBF = 69888
A_WH = 72000
A_WO = 88384
A_UG = 0
A_YFM = 65536
A_W1R = 81920
A_W2R = 90112
A_BYTES = 106496
DEBUG = False


class Ctr:
    def __init__(self, es, nc, name):
        self.h = es.enter_context(nc.semaphore(name))
        self.n = 0

    def inc(self, instr, k=1):
        instr.then_inc(self.h, k)
        self.n += k
        return self.n

    def reserve(self, k=1):
        self.n += k
        return self.n


class DeferredScalar:

    def __init__(self):
        self.fns = []

    def add(self, fn):
        self.fns.append(fn)

    def emit(self, block):
        fns, self.fns = self.fns, []
        if fns:
            def body(sc):
                for f in fns:
                    f(sc)
            block.scalar(body)


def build_nc():
    nc = bass.Bass("TRN2", target_bir_lowering=False, debug=False)

    x_d = nc.dram_tensor("x", [T, HID], F32, kind="ExternalInput")
    wq_d = nc.dram_tensor("wq", [HID, HID], F32, kind="ExternalInput")
    wk_d = nc.dram_tensor("wk", [HID, HID], F32, kind="ExternalInput")
    wv_d = nc.dram_tensor("wv", [HID, HID], F32, kind="ExternalInput")
    wh_d = nc.dram_tensor("wh", [HID, HID], F32, kind="ExternalInput")
    wo_d = nc.dram_tensor("wo", [HID, HID], F32, kind="ExternalInput")
    bh_d = nc.dram_tensor("bh", [1, HID], F32, kind="ExternalInput")
    beta_d = nc.dram_tensor("beta", [1, HID], F32, kind="ExternalInput")
    bo_d = nc.dram_tensor("bo", [1, HID], F32, kind="ExternalInput")
    wrt_d = nc.dram_tensor("wrt", [HID, 8], F32, kind="ExternalInput")
    w1_d = nc.dram_tensor("w1", [HID, FF], F32, kind="ExternalInput")
    w2_d = nc.dram_tensor("w2", [FF, HID], F32, kind="ExternalInput")
    idn_d = nc.dram_tensor("idn_h", [128, 128], F32, kind="ExternalInput")
    out_d = nc.dram_tensor("out", [T, HID], F32, kind="ExternalOutput")
    dbg = {}
    if DEBUG:
        dbg["d_kv"] = nc.dram_tensor("d_kv", [64, HEADS * 65], F32, kind="ExternalOutput")
        dbg["d_kvbf"] = nc.dram_tensor("d_kvbf", [128, HEADS * 65], BF16, kind="ExternalOutput")
        dbg["d_ln1"] = nc.dram_tensor("d_ln1", [128, NCH * T], BF16, kind="ExternalOutput")
        dbg["d_q"] = nc.dram_tensor("d_q", [128, NCH * T], BF16, kind="ExternalOutput")
        dbg["d_attn0"] = nc.dram_tensor("d_attn0", [128, 1040], F32, kind="ExternalOutput")
        dbg["d_r2"] = nc.dram_tensor("d_r2", [128, NBLK * HID], F32, kind="ExternalOutput")
        dbg["d_p7"] = nc.dram_tensor("d_p7", [128, NBLK], F32, kind="ExternalOutput")
        dbg["d_y"] = nc.dram_tensor("d_y", [128, NCH * 1024], BF16, kind="ExternalOutput")
        dbg["d_hpc0"] = nc.dram_tensor("d_hpc0", [128, 1024], BF16, kind="ExternalOutput")
        dbg["d_q2"] = nc.dram_tensor("d_q2", [128, 2048], BF16, kind="ExternalOutput")
        dbg["d_cb"] = nc.dram_tensor("d_cb", [128, 1024], BF16, kind="ExternalOutput")
        dbg["d_g"] = nc.dram_tensor("d_g", [128, 4096], BF16, kind="ExternalOutput")
        dbg["d_ao0"] = nc.dram_tensor("d_ao0", [128, 1024], F32, kind="ExternalOutput")
        dbg["d_gtm0"] = nc.dram_tensor("d_gtm0", [128, 1024], BF16, kind="ExternalOutput")
        dbg["d_lnat0"] = nc.dram_tensor("d_lnat0", [128, 1040], F32, kind="ExternalOutput")
        dbg["d_stat0"] = nc.dram_tensor("d_stat0", [128, 8], F32, kind="ExternalOutput")
        dbg["d_wq0"] = nc.dram_tensor("d_wq0", [128, 1024], BF16, kind="ExternalOutput")
    cc_in = nc.dram_tensor("cc_in", [64, HEADS * 65], BF16, kind="Internal")
    cc_out = nc.dram_tensor("cc_out", [64, HEADS * 65], BF16, kind="Internal")

    es = contextlib.ExitStack()
    x_tm = es.enter_context(nc.sbuf_tensor("x_tm", [128, NBLK * HID], F32))
    lnx_fm = es.enter_context(nc.sbuf_tensor("lnx_fm", [128, NCH * T], BF16))
    arena = es.enter_context(nc.sbuf_tensor("arena", [128, A_BYTES // 4], F32))
    idn = es.enter_context(nc.sbuf_tensor("idnf", [128, 128], F32))
    idn_bf = es.enter_context(nc.sbuf_tensor("idnb", [128, 128], BF16))
    wrt_bf = es.enter_context(nc.sbuf_tensor("wrtb", [128, 64], BF16))
    scr2 = es.enter_context(nc.sbuf_tensor("scr2", [128, 512], F32))
    eT_sb = es.enter_context(nc.sbuf_tensor("eT_sb", [128, 48], F32))
    p7_tm = es.enter_context(nc.sbuf_tensor("p7_tm", [128, NBLK], F32))
    sq_scr = es.enter_context(nc.sbuf_tensor("sq_scr", [128, HID], BF16))
    stat = es.enter_context(nc.sbuf_tensor("stat", [128, 8], F32))
    sp1 = es.enter_context(nc.sbuf_tensor("sp1", [128, 96], F32))
    ones_bf = es.enter_context(nc.sbuf_tensor("ones_bf", [128, 1], BF16))
    eps_t = es.enter_context(nc.sbuf_tensor("eps_t", [128, 1], F32))
    nh_t = es.enter_context(nc.sbuf_tensor("nh_t", [128, 1], F32))

    S_MU, S_SS, S_VAR, S_S, S_NEGMS, S_T0, S_T1 = 0, 1, 2, 3, 4, 5, 6

    pp = [es.enter_context(nc.psum_tensor(f"pp{i}", [128, 512], F32)) for i in range(8)]

    def ap_f32(off, rows, cols):
        return arena[0:rows, off // 4: off // 4 + cols]

    def ap_bf16(off, rows, cols):
        return arena[:, off // 4: off // 4 + (cols + 1) // 2].bitcast(BF16)[0:rows, 0:cols]

    def wk_bf(c):
        return ap_bf16(A_WK + c * 2048, 128, 1024)

    def wv_bf(c):
        return ap_bf16(A_WV + c * 2048, 128, 1024)

    def wh_bf(c):
        return ap_bf16(A_WH + c * 2048, 128, 1024)

    def wq_bf(c):
        return ap_bf16(A_WQ + c * 2048, 128, 1024)

    def wo_bf(c):
        return ap_bf16(A_WO + c * 2048, 128, 1024)

    def ktm(i):
        return ap_bf16(A_KTM + (i % 2) * 2048, 128, 1024)

    def vtm(i):
        return ap_bf16(A_VTM + (i % 2) * 2048, 128, 1024)

    def ln1tm(i):
        return ap_bf16(A_LN1TM + (i % 2) * 2048, 128, 1024)

    def qbf():
        return ap_bf16(A_QBF, 128, 16384)

    _HPC = [A_HPC, A_HPC + 2048, A_NUM, A_NUM + 2048]

    def hpc(b):
        return ap_bf16(_HPC[b % 4], 128, 1024)

    def atm():
        return ap_f32(A_ATM, 128, 4160)

    def gfm():
        return ap_bf16(A_GFM, 128, 4096)

    def numsl(j):
        return ap_f32(A_NUM + (j % 2) * 2048, 128, 512)

    def gtm(b):
        if b % 2 == 0:
            return sq_scr[:, 0:1024]
        return scr2[:].bitcast(BF16)[0:128, 0:1024]

    def e_sb():
        return scr2[0:8, :]

    cb_b = ap_bf16(A_CBB, 128, 1024)
    bo_b = ap_bf16(A_BOB, 128, 1024)
    kvbf = ap_bf16(A_KVBF, 128, 1040)
    kvf32 = ap_bf16(A_KVF, 64, 1040)
    bias_t1 = ap_f32(A_ATM, 128, 1024)
    bias_t2 = ap_f32(A_ATM + 4096, 128, 1024)

    def ug(m):
        return ap_bf16(A_UG + m * 2048, 128, 1024)

    def yfm(j):
        return ap_bf16(A_YFM + j * 2048, 128, 1024)

    def w1r(m):
        return ap_bf16(A_W1R + (m % 4) * 2048, 128, 1024)

    def w2r(j):
        return ap_bf16(A_W2R + (j % 2) * 8192, 128, 4096)

    rc_sb = es.enter_context(nc.sbuf_tensor("rc_sb", [128, 64], F32))
    c_te = Ctr(es, nc, "c_te")
    c_ve = Ctr(es, nc, "c_ve")
    c_se = Ctr(es, nc, "c_se")
    s_sd = Ctr(es, nc, "s_sd")
    s_x = Ctr(es, nc, "s_x")
    s_st = Ctr(es, nc, "s_st")
    s_wl = Ctr(es, nc, "s_wl")
    s_sq = Ctr(es, nc, "s_sq")
    s_cc = Ctr(es, nc, "s_cc")
    s_dbg = Ctr(es, nc, "s_dbg")

    lnc = None

    with nc.Block() as block:
        lnc_view = lnx_fm[:].rearrange("p (c n) -> p c n", c=NCH)

        X_Q = []

        def init_sync(sync):
            xr = x_d.ap().rearrange("(t p) k -> p t k", p=128)
            xo = x_tm[:].rearrange("p (t k) -> p t k", t=NBLK)
            for qtr in range(4):
                s_x.inc(sync.dma_start(out=xo[:, qtr * 4:(qtr + 1) * 4, :],
                                       in_=xr[:, qtr * 4:(qtr + 1) * 4, :]), 16)
                X_Q.append(s_x.n)
            s_x.inc(sync.dma_start(out=idn[:], in_=idn_d.ap()), 16)
        block.sync(init_sync)
        X_LOADED = s_x.n

        def init_gp(gp):
            s_wl.inc(gp.dma_start(out=bias_t1, in_=bh_d.ap().partition_broadcast(128)), 16)
            s_wl.inc(gp.dma_start(out=bias_t2, in_=beta_d.ap().partition_broadcast(128)), 16)
            s_wl.inc(gp.dma_start(out=bo_b, in_=bo_d.ap().partition_broadcast(128)), 16)
            s_wl.inc(gp.dma_start(
                out=ap_bf16(A_WK, 128, NCH * 1024).rearrange("p (c m) -> p c m", c=NCH),
                in_=wk_d.ap().rearrange("(c k) m -> k c m", c=NCH)), 16)
            s_wl.inc(gp.dma_start(
                out=ap_bf16(A_WV, 128, NCH * 1024).rearrange("p (c m) -> p c m", c=NCH),
                in_=wv_d.ap().rearrange("(c k) m -> k c m", c=NCH)), 16)
            s_wl.inc(gp.dma_start(
                out=ap_bf16(A_WH, 128, NCH * 1024).rearrange("p (c m) -> p c m", c=NCH),
                in_=wh_d.ap().rearrange("(c k) m -> k c m", c=NCH)), 16)
            gp.wait_ge(s_x.h, X_LOADED)
            s_wl.inc(gp.dma_start(out=idn_bf[:], in_=idn[:]), 16)
            s_wl.inc(gp.dma_start(
                out=wrt_bf[:].rearrange("p (c e) -> p c e", c=NCH),
                in_=wrt_d.ap().rearrange("(c k) e -> k c e", c=NCH)), 16)
        block.gpsimd(init_gp)
        BIAS_LOADED = 32
        P1W_LOADED = s_wl.n

        def init_vec(vec):
            vec.memset(ones_bf[:], 1.0)
            vec.memset(eps_t[:], EPS)
            vec.memset(nh_t[:], -0.5)
            vec.wait_ge(s_wl.h, P1W_LOADED)
            c_ve.inc(vec.tensor_tensor(cb_b, bias_t1, bias_t2, ALU.add))
        block.vector(init_vec)
        CB_DONE = c_ve.n

        def p1_pre_v(vec):
            vec.wait_ge(s_x.h, X_LOADED)
        block.vector(p1_pre_v)

        def p1_pre_s(sc):
            sc.wait_ge(s_x.h, X_LOADED)
        block.scalar(p1_pre_s)

        def p1_stats_v(vec):
            for i in range(NBLK):
                if i % 4 == 0:
                    vec.wait_ge(s_x.h, X_Q[i // 4])
                vec.reduce_sum(sp1[:, i:i + 1], x_tm[:, i * HID:(i + 1) * HID], axis=AX)
        block.vector(p1_stats_v)

        def p1_stats_s(sc):
            last = None
            for i in range(NBLK):
                if i % 4 == 0:
                    sc.wait_ge(s_x.h, X_Q[i // 4])
                last = sc.activation(sq_scr[:, 0:1024], x_tm[:, i * HID:(i + 1) * HID],
                                     ACT.Square, bias=0.0, scale=1.0,
                                     accum_out=sp1[:, 16 + i:17 + i])
            c_se.inc(last)
        block.scalar(p1_stats_s)
        P1_SS = c_se.n

        def p1_chain_v(vec):
            vec.wait_ge(c_se.h, P1_SS)
            vec.drain()
            vec.tensor_scalar(sp1[:, 32:48], sp1[:, 16:32], 1.0 / HID, None, ALU.mult)
            vec.tensor_scalar(sp1[:, 0:16], sp1[:, 0:16], 1.0 / HID, None, ALU.mult)
            vec.drain()
            vec.tensor_tensor(sp1[:, 48:64], sp1[:, 0:16], sp1[:, 0:16], ALU.mult)
            vec.drain()
            c_ve.inc(vec.scalar_tensor_tensor(sp1[:, 48:64], sp1[:, 32:48], EPS,
                                              sp1[:, 48:64], ALU.add, ALU.subtract))
        block.vector(p1_chain_v)
        P1_VAR = c_ve.n

        def p1_chain_s(sc):
            sc.wait_ge(c_ve.h, P1_VAR)
            c_se.inc(sc.activation(sp1[:, 32:48], sp1[:, 48:64], ACT.Sqrt,
                                   bias=0.0, scale=1.0))
        block.scalar(p1_chain_s)
        P1_SA = c_se.n

        def p1_negms_v(vec):
            vec.wait_ge(c_se.h, P1_SA)
            vec.reciprocal(sp1[:, 64:80], sp1[:, 32:48])
            vec.drain()
            c_ve.inc(vec.scalar_tensor_tensor(sp1[:, 80:96], sp1[:, 0:16], -1.0,
                                              sp1[:, 64:80], ALU.mult, ALU.mult))
        block.vector(p1_negms_v)
        P1_NEG = c_ve.n

        te_kv = [0] * NBLK
        se_kevac = [0] * NBLK
        ve_vevac = [0] * NBLK
        ve_lnfm = [0] * NBLK
        lnT_done_list = [0] * NBLK
        dsc = DeferredScalar()

        def p1_apply(i):
            xblk = x_tm[:, i * HID:(i + 1) * HID]

            def v_apply(vec, i=i, xblk=xblk, pT=(lnT_done_list[i - 2] if i >= 2 else 0)):
                if pT:
                    vec.wait_ge(c_te.h, pT)
                c_ve.inc(vec.tensor_scalar(ln1tm(i)[:, :], xblk,
                                           sp1[:, 64 + i:65 + i],
                                           sp1[:, 80 + i:81 + i], ALU.mult, ALU.add))
            block.vector(v_apply)
            return c_ve.n

        def p1_trans(i, ad):
            def t_trans(t, i=i, ad=ad, prev_ev=(ve_lnfm[i - 1] if i > 0 else 0)):
                t.wait_ge(c_ve.h, ad)
                if i > 0:
                    t.wait_ge(c_ve.h, prev_ev)
                dst = pp[7].ap().bitcast(BF16)
                last = None
                for c in range(NCH):
                    last = t.transpose(dst[:, c * 128:(c + 1) * 128],
                                       ln1tm(i)[:, c * 128:(c + 1) * 128], idn_bf[:])
                c_te.inc(last)
            block.tensor(t_trans)
            lnT_done_list[i] = c_te.n
            return c_te.n

        def p1_evac_ln(i, td):
            def v_evac_ln(vec, i=i, td=td):
                vec.wait_ge(c_te.h, td)
                src = pp[7].ap().bitcast(BF16)
                c_ve.inc(vec.tensor_copy(
                    lnc_view[:, :, i * 128:(i + 1) * 128],
                    src[:].rearrange("p (c n) -> p c n", c=NCH)))
            block.vector(v_evac_ln)
            ve_lnfm[i] = c_ve.n

        ad0 = p1_apply(0)
        td0 = p1_trans(0, ad0)
        p1_evac_ln(0, td0)
        for i in range(NBLK):
            def t_kproj(t, i=i, lev=ve_lnfm[i]):
                t.wait_ge(c_ve.h, lev)
                if i == 0:
                    t.wait_ge(s_wl.h, P1W_LOADED)
                if i >= 1:
                    t.wait_ge(c_se.h, se_kevac[i - 1])
                last = None
                for c in range(NCH):
                    lhs = lnc_view[:, c, i * 128:(i + 1) * 128]
                    t.matmul(pp[3][:, :], lhs, wk_bf(c)[:, 0:512],
                             start=(c == 0), stop=(c == NCH - 1))
                    last = t.matmul(pp[4][:, :], lhs, wk_bf(c)[:, 512:1024],
                                    start=(c == 0), stop=(c == NCH - 1))
                c_te.inc(last)
            block.tensor(t_kproj)
            kproj_done = c_te.n

            def t_vproj(t, i=i, pv=(ve_vevac[i - 1] if i >= 1 else 0)):
                if pv:
                    t.wait_ge(c_ve.h, pv)
                last = None
                for c in range(NCH):
                    lhs = lnc_view[:, c, i * 128:(i + 1) * 128]
                    t.matmul(pp[5][:, :], lhs, wv_bf(c)[:, 0:512],
                             start=(c == 0), stop=(c == NCH - 1))
                    last = t.matmul(pp[6][:, :], lhs, wv_bf(c)[:, 512:1024],
                                    start=(c == 0), stop=(c == NCH - 1))
                c_te.inc(last)
            block.tensor(t_vproj)
            vproj_done = c_te.n

            if i + 1 < NBLK:
                ad = p1_apply(i + 1)
                td = p1_trans(i + 1, ad)

            se_kevac[i] = c_se.reserve(1)

            def s_kevac(sc, i=i, kd=kproj_done, tk=(te_kv[i - 2] if i >= 2 else 0)):
                sc.wait_ge(c_te.h, kd)
                if tk:
                    sc.wait_ge(c_te.h, tk)
                sc.activation(ktm(i)[:, 0:512], pp[3][:, :], ACT.Sigmoid, bias=0.0, scale=1.0)
                sc.activation(ktm(i)[:, 512:1024], pp[4][:, :], ACT.Sigmoid,
                              bias=0.0, scale=1.0).then_inc(c_se.h, 1)
            dsc.add(s_kevac)

            def v_vevac(vec, i=i, kd=vproj_done):
                vec.wait_ge(c_te.h, kd)
                if i >= 2:
                    vec.wait_ge(c_te.h, te_kv[i - 2])
                vec.tensor_copy(vtm(i)[:, 0:512], pp[5][:, :])
                c_ve.inc(vec.tensor_copy(vtm(i)[:, 512:1024], pp[6][:, :]))
            block.vector(v_vevac)
            ve_vevac[i] = c_ve.n

            if i + 1 < NBLK:
                p1_evac_ln(i + 1, td)

            def t_kv(t, i=i, ke=se_kevac[i], ve=ve_vevac[i]):
                t.wait_ge(c_se.h, ke)
                t.wait_ge(c_ve.h, ve)
                last = None
                for h in range(HEADS):
                    lhs = ktm(i)[:, h * DH:(h + 1) * DH]
                    bank = pp[0] if h < 8 else pp[1]
                    col = (h % 8) * DH
                    t.matmul(bank[0:DH, col:col + DH], lhs, vtm(i)[:, h * DH:(h + 1) * DH],
                             start=(i == 0 and h % 8 == 0), stop=(i == NBLK - 1),
                             skip_group_check=True)
                    last = t.matmul(pp[2][0:DH, h:h + 1], lhs, ones_bf[:],
                                    start=(i == 0 and h == 0), stop=(i == NBLK - 1),
                                    skip_group_check=True)
                c_te.inc(last)
            block.tensor(t_kv)
            te_kv[i] = c_te.n

        if DEBUG:
            def dbg_ln1(sync):
                sync.wait_ge(c_ve.h, ve_lnfm[NBLK - 1])
                s_dbg.inc(sync.dma_start(out=dbg["d_ln1"].ap(), in_=lnx_fm[:]), 16)
            block.sync(dbg_ln1)
        DBG_LN1 = s_dbg.n

        dsc.emit(block)

        def v_kv_evac(vec):
            vec.wait_ge(c_te.h, te_kv[NBLK - 1])
            kvr = kvf32.rearrange("p (h j) -> p h j", h=HEADS)
            vec.tensor_copy(kvr[:, 0:8, 0:DH],
                            pp[0][0:DH, :].rearrange("p (h j) -> p h j", h=8))
            vec.tensor_copy(kvr[:, 8:16, 0:DH],
                            pp[1][0:DH, :].rearrange("p (h j) -> p h j", h=8))
            c_ve.inc(vec.tensor_copy(kvr[:, :, DH:DH + 1],
                                     pp[2][0:DH, 0:HEADS].unsqueeze(2)))
        block.vector(v_kv_evac)
        KV_EVAC = c_ve.n
        if DEBUG:
            def dbg_kv(sync):
                sync.wait_ge(c_ve.h, KV_EVAC)
                s_dbg.inc(sync.dma_start(out=dbg["d_kv"].ap(), in_=kvf32), 16)
            block.sync(dbg_kv)

        def sc_ccin(sc):
            sc.wait_ge(c_ve.h, KV_EVAC)
            s_sq.inc(sc.dma_start(out=cc_in.ap(), in_=kvf32), 16)
        block.scalar(sc_ccin)
        CCIN = s_sq.n

        def gp_wq(gp):
            s_wl.inc(gp.dma_start(
                out=ap_bf16(A_WQ, 128, NCH * 1024).rearrange("p (c m) -> p c m", c=NCH),
                in_=wq_d.ap().rearrange("(c k) m -> k c m", c=NCH)), 16)
        block.gpsimd(gp_wq)
        WQ_LOADED = s_wl.n

        def gp_cc(gp):
            gp.wait_ge(s_sq.h, CCIN)
            s_cc.inc(gp.collective_compute(
                "AllReduce", ALU.add,
                replica_groups=[list(range(NCORES))],
                ins=[cc_in.ap().opt()],
                outs=[cc_out.ap().opt()]))
            gp.wait_ge(s_cc.h, s_cc.n)
            s_wl.inc(gp.dma_start(out=kvbf[0:64, :], in_=cc_out.ap()), 16)
        block.gpsimd(gp_cc)

        def sy_kvdup(sync):
            sync.wait_ge(s_cc.h, s_cc.n)
            s_wl.inc(sync.dma_start(out=kvbf[64:128, :], in_=cc_out.ap()), 16)
        block.sync(sy_kvdup)

        def gp_wo(gp):
            s_wl.inc(gp.dma_start(
                out=ap_bf16(A_WO, 128, NCH * 1024).rearrange("p (c m) -> p c m", c=NCH),
                in_=wo_d.ap().rearrange("(c k) m -> k c m", c=NCH)), 16)
        block.gpsimd(gp_wo)
        KV_READY = s_wl.n - 16
        if DEBUG:
            def dbg_kvbf(gp, kr=s_wl.n - 16):
                gp.wait_ge(s_wl.h, kr)
                s_dbg.inc(gp.dma_start(out=dbg["d_kvbf"].ap(), in_=kvbf[:, :]), 16)
            block.gpsimd(dbg_kvbf)
        WO_LOADED = s_wl.n

        se_qevac = [0] * NCH
        for m in range(NCH):
            def t_q(t, m=m):
                if m == 0:
                    t.wait_ge(s_wl.h, WQ_LOADED)
                    t.wait_ge(c_ve.h, KV_EVAC)
                    t.wait_ge(c_se.h, se_kevac[NBLK - 1])
                if m >= 2:
                    t.wait_ge(c_se.h, se_qevac[m - 2])
                banks = pp[0:4] if m % 2 == 0 else pp[4:8]
                last = None
                for c in range(NCH):
                    lhsT = wq_bf(c)[:, m * 128:(m + 1) * 128]
                    for seg in range(4):
                        last = t.matmul(banks[seg][:, :], lhsT,
                                        lnc_view[:, c, seg * 512:(seg + 1) * 512],
                                        start=(c == 0), stop=(c == NCH - 1))
                c_te.inc(last)
            block.tensor(t_q)
            q_done_m = c_te.n

            se_qevac[m] = c_se.reserve(1)

            def s_qevac(sc, m=m, qd=q_done_m):
                sc.wait_ge(c_te.h, qd)
                banks = pp[0:4] if m % 2 == 0 else pp[4:8]
                last = None
                for seg in range(4):
                    last = sc.activation(qbf()[:, m * T + seg * 512:m * T + (seg + 1) * 512],
                                         banks[seg][:, :], ACT.Sigmoid, bias=0.0, scale=1.0)
                last.then_inc(c_se.h, 1)
            dsc.add(s_qevac)
            if DEBUG and m == 0:
                def dbg_q2(sync, qe=c_se.n):
                    sync.wait_ge(c_se.h, qe)
                    s_dbg.inc(sync.dma_start(out=dbg["d_q2"].ap(), in_=qbf()[:, 0:2048]), 16)
                    s_dbg.inc(sync.dma_start(out=dbg["d_wq0"].ap(), in_=wq_bf(0)), 16)
                block.sync(dbg_q2)
        Q_DONE = se_qevac[NCH - 1]
        if DEBUG:
            def dbg_q(sync):
                sync.wait_ge(c_se.h, Q_DONE)
                s_dbg.inc(sync.dma_start(out=dbg["d_q"].ap(), in_=qbf()), 16)
            block.sync(dbg_q)
        DBG_Q = s_dbg.n

        NSTRIP = 4
        ve_r2 = [0] * NBLK
        ve_ln2fm = [0] * NBLK
        prev_h_evac = 0
        prev_num_evac = {}
        prev_T_evac = {}
        prev_gTe = {}
        prev_lnTe = {}
        strip_T_done = [0] * NSTRIP
        ao1_done_l = {}
        gated_last = [0]
        dsc2 = DeferredScalar()

        def emit_num_phase(s, inject=None):
            tgt = dsc if inject is not None else dsc2

            def emit_T(h):
                def t_numT(t, s=s, h=h, nev=prev_num_evac[(s, h)],
                           pg=(prev_gTe.get((s - 1, 2 + h % 2), 0) if h < 2 else 0)):
                    t.wait_ge(c_se.h, nev)
                    key = (s, h - 2) if h >= 2 else (s - 1, h + HEADS - 2)
                    if key in prev_T_evac:
                        t.wait_ge(c_ve.h, prev_T_evac[key])
                    if pg:
                        t.wait_ge(c_ve.h, pg)
                    bank = pp[6 + (h % 2)]
                    last = None
                    for b4 in range(4):
                        last = t.transpose(bank[:, b4 * 65:b4 * 65 + 65],
                                           numsl(h)[0:65, b4 * 128:(b4 + 1) * 128],
                                           idn[0:65, 0:65])
                    c_te.inc(last)
                block.tensor(t_numT)
                numT_done = c_te.n

                def v_tev(vec, h=h, td=numT_done):
                    vec.wait_ge(c_te.h, td)
                    bank = pp[6 + (h % 2)]
                    b3 = bank[:, 0:260].rearrange("p (b j) -> p b j", b=4)
                    vec.reciprocal(rc_sb[:, h * 4:h * 4 + 4], b3[:, :, 64:65].opt())
                    vec.drain()
                    last = None
                    for b4 in range(4):
                        last = vec.tensor_scalar(
                            atm()[:, b4 * 1040 + h * 65:b4 * 1040 + (h + 1) * 65],
                            bank[:, b4 * 65:(b4 + 1) * 65],
                            rc_sb[:, h * 4 + b4:h * 4 + b4 + 1], None, ALU.mult)
                    c_ve.inc(last)
                block.vector(v_tev)
                prev_T_evac[(s, h)] = c_ve.n

            for h in range(HEADS):
                def t_num(t, s=s, h=h):
                    if s == 0 and h == 0:
                        t.wait_ge(s_wl.h, KV_READY + 16)
                        t.wait_ge(c_se.h, Q_DONE)
                    key = (s, h - 2) if h >= 2 else (s - 1, h + HEADS - 2)
                    if key in prev_num_evac:
                        t.wait_ge(c_se.h, prev_num_evac[key])
                    base = (h % 2) * 64
                    bank = pp[4 + (h % 2)]
                    c_te.inc(t.matmul(
                        bank[0:65, :],
                        kvbf[base:base + 64, h * 65:(h + 1) * 65],
                        qbf()[base:base + 64,
                              (h // 2) * T + s * 512:(h // 2) * T + (s + 1) * 512],
                        start=True, stop=True))
                block.tensor(t_num)
                num_done = c_te.n
                prev_num_evac[(s, h)] = c_se.reserve(1)

                def s_nev(sc, s=s, h=h, nd=num_done,
                          gw=(gated_last[0] if (s > 0 and h == 0) else 0)):
                    sc.wait_ge(c_te.h, nd)
                    if gw:
                        sc.wait_ge(c_ve.h, gw)
                    sc.copy(numsl(h)[0:65, :], pp[4 + (h % 2)][0:65, :]).then_inc(c_se.h, 1)
                tgt.add(s_nev)
                if h >= 1:
                    emit_T(h - 1)
                if inject and h in inject:
                    for fn in inject[h]:
                        fn()
            emit_T(HEADS - 1)
            strip_T_done[s] = prev_T_evac[(s, HEADS - 1)]

        pre_th = {}
        for pbi in range(2):
            def t_h_pre(t, bi=pbi):
                if bi == 0:
                    t.wait_ge(c_se.h, se_qevac[NCH - 2])
                    t.wait_ge(c_ve.h, CB_DONE)
                pb0, pb1 = (0, 1) if bi % 2 == 0 else (2, 3)
                last = None
                for c in range(NCH):
                    lhs = lnc_view[:, c, bi * 128:(bi + 1) * 128]
                    t.matmul(pp[pb0][:, :], lhs, wh_bf(c)[:, 0:512],
                             start=(c == 0), stop=(c == NCH - 1))
                    last = t.matmul(pp[pb1][:, :], lhs, wh_bf(c)[:, 512:1024],
                                    start=(c == 0), stop=(c == NCH - 1))
                c_te.inc(last)
            block.tensor(t_h_pre)
            pre_th[pbi] = c_te.n

        emit_num_phase(0)
        for s in range(NSTRIP):
            blks = [s * 4 + j for j in range(4)]

            vh_l = {}
            lss_box = [0]

            def mk_t_h(bi):
                b = blks[bi]

                def t_h(t, s=s, b=b, bi=bi,
                        pl=(prev_lnTe.get((s - 1, 3), 0) if (bi == 0 and s > 0) else 0),
                        pa=(ao1_done_l.get((s - 1, 3), 0) if (bi == 1 and s > 0) else 0),
                        vh=vh_l.get(bi - 2, 0)):
                    if s == 0 and bi == 0:
                        t.wait_ge(c_se.h, Q_DONE)
                        t.wait_ge(c_ve.h, CB_DONE)
                    if pl:
                        t.wait_ge(c_ve.h, pl)
                    if pa:
                        t.wait_ge(c_ve.h, pa)
                    if vh:
                        t.wait_ge(c_ve.h, vh)
                    p0, p1 = (0, 1) if bi % 2 == 0 else (2, 3)
                    last = None
                    for c in range(NCH):
                        lhs = lnc_view[:, c, b * 128:(b + 1) * 128]
                        t.matmul(pp[p0][:, :], lhs, wh_bf(c)[:, 0:512],
                                 start=(c == 0), stop=(c == NCH - 1))
                        last = t.matmul(pp[p1][:, :], lhs, wh_bf(c)[:, 512:1024],
                                        start=(c == 0), stop=(c == NCH - 1))
                    c_te.inc(last)
                block.tensor(t_h)
                return c_te.n

            def mk_v_h(bi, hd):
                b = blks[bi]

                def v_h(vec, b=b, bi=bi, hd=hd):
                    p0, p1 = (0, 1) if bi % 2 == 0 else (2, 3)
                    vec.wait_ge(c_te.h, hd)
                    vec.tensor_tensor(hpc(b)[:, 0:512], pp[p0][:, :], cb_b[:, 0:512], ALU.add)
                    c_ve.inc(vec.tensor_tensor(hpc(b)[:, 512:1024], pp[p1][:, :],
                                               cb_b[:, 512:1024], ALU.add))
                block.vector(v_h)
                vh_l[bi] = c_ve.n
                return c_ve.n

            th0 = pre_th[0] if s == 0 else mk_t_h(0)
            th1 = pre_th[1] if s == 0 else mk_t_h(1)

            def vA(vec, s=s):
                vec.drain()
                for bi in range(4):
                    at3 = atm()[:, bi * 1040:bi * 1040 + 1040].rearrange(
                        "p (h j) -> p h j", h=HEADS)
                    vec.reduce_sum(sp1[:, bi:bi + 1], at3[:, :, 0:64], axis=AXY)
                c_ve.inc(vec.drain())
            block.vector(vA)
            DIVRED = c_ve.n
            mk_v_h(0, th0)
            mk_v_h(1, th1)
            th2 = mk_t_h(2)
            th3 = mk_t_h(3)
            for f in dsc2.fns:
                dsc.add(f)
            dsc2.fns = []
            ASS_DONE = c_se.reserve(1)

            def sA(sc, s=s, dr=DIVRED):
                sc.wait_ge(c_ve.h, dr)
                last = None
                for bi in range(4):
                    at_str = atm()[:, bi * 1040:bi * 1040 + 1040].rearrange(
                        "p (h j) -> p h j", h=HEADS)[:, :, 0:64]
                    last = sc.activation(
                        gfm()[:, 0:1024].rearrange("p (h j) -> p h j", h=HEADS),
                        at_str, ACT.Square, bias=0.0, scale=1.0,
                        accum_out=sp1[:, 4 + bi:5 + bi])
                last.then_inc(c_se.h, 1)
            dsc.add(sA)

            def vB(vec):
                vec.wait_ge(c_se.h, ASS_DONE)
                vec.drain()
                vec.tensor_scalar(sp1[:, 8:12], sp1[:, 4:8], 1.0 / HID, None, ALU.mult)
                vec.tensor_scalar(sp1[:, 0:4], sp1[:, 0:4], 1.0 / HID, None, ALU.mult)
                vec.drain()
                vec.tensor_tensor(sp1[:, 12:16], sp1[:, 0:4], sp1[:, 0:4], ALU.mult)
                vec.drain()
                c_ve.inc(vec.scalar_tensor_tensor(sp1[:, 8:12], sp1[:, 8:12], EPS,
                                                  sp1[:, 12:16], ALU.add, ALU.subtract))
            block.vector(vB)
            AVAR_DONE = c_ve.n
            mk_v_h(2, th2)
            ASD_DONE = c_se.reserve(1)

            def sB(sc, av=AVAR_DONE):
                sc.wait_ge(c_ve.h, av)
                sc.activation(sp1[:, 12:16], sp1[:, 8:12], ACT.Sqrt,
                              bias=0.0, scale=1.0).then_inc(c_se.h, 1)
            dsc.add(sB)

            def vC(vec):
                vec.wait_ge(c_se.h, ASD_DONE)
                vec.reciprocal(sp1[:, 16:20], sp1[:, 12:16])
                vec.drain()
                c_ve.inc(vec.scalar_tensor_tensor(sp1[:, 20:24], sp1[:, 0:4], -1.0,
                                                  sp1[:, 16:20], ALU.mult, ALU.mult))
            block.vector(vC)
            ASTATS = c_ve.n
            mk_v_h(3, th3)
            prev_h_evac = vh_l[3]

            gT_done_l = {}
            wo_done_l = {}

            def mk_ao1(bi):
                boff = bi * 1040

                def v_ao1(vec, bi=bi, wd=wo_done_l[bi], boff=boff):
                    vec.wait_ge(c_te.h, wd)
                    t1 = atm()[:, boff:boff + 1024]
                    vec.tensor_tensor(t1[:, 0:512], pp[2][:, :], bo_b[:, 0:512], ALU.add)
                    c_ve.inc(vec.tensor_tensor(t1[:, 512:1024], pp[3][:, :],
                                               bo_b[:, 512:1024], ALU.add))
                block.vector(v_ao1)
                ao1_done_l[(s, bi)] = c_ve.n

            def mk_r2m(bi):
                b = blks[bi]
                boff = bi * 1040

                def v_r2(vec, b=b, boff=boff):
                    vec.drain()
                    xb = x_tm[:, b * HID:(b + 1) * HID]
                    c_ve.inc(vec.scalar_tensor_tensor(
                        xb, atm()[:, boff:boff + 1024], 0.0, xb, ALU.max, ALU.add))
                block.vector(v_r2)
                ve_r2[b] = c_ve.n

                def v_mean(vec, b=b, bi=bi):
                    vec.drain()
                    c_ve.inc(vec.reduce_sum(sp1[:, 32 + bi:33 + bi],
                                            x_tm[:, b * HID:(b + 1) * HID], axis=AX))
                block.vector(v_mean)
                if bi == 3:
                    lss_box[0] = c_se.reserve(1)

                def sC_b(sc, b=b, bi=bi, rw=ve_r2[b], inc=(bi == 3)):
                    sc.wait_ge(c_ve.h, rw)
                    a = sc.activation(atm()[:, bi * 1040:bi * 1040 + 1024],
                                      x_tm[:, b * HID:(b + 1) * HID],
                                      ACT.Square, bias=0.0, scale=1.0,
                                      accum_out=sp1[:, 36 + bi:37 + bi])
                    if inc:
                        a.then_inc(c_se.h, 1)
                dsc.add(sC_b)

            for bi, b in enumerate(blks):
                boff = bi * 1040
                at_str = atm()[:, boff:boff + 1040].rearrange(
                    "p (h j) -> p h j", h=HEADS)[:, :, 0:64]

                def v_apply_gate(vec, s=s, b=b, bi=bi, at_str=at_str,
                                 gtp=gT_done_l.get(bi - 2, 0)):
                    if gtp:
                        vec.wait_ge(c_te.h, gtp)
                    vec.tensor_scalar(at_str, at_str, sp1[:, 16 + bi:17 + bi],
                                      sp1[:, 20 + bi:21 + bi], ALU.mult, ALU.add)
                    vec.drain()
                    c_ve.inc(vec.tensor_tensor(
                        gtm(bi)[:, :].rearrange("p (h j) -> p h j", h=HEADS),
                        at_str,
                        hpc(b)[:].rearrange("p (h j) -> p h j", h=HEADS),
                        ALU.mult))
                block.vector(v_apply_gate)
                gated_done = c_ve.n
                if bi == 3:
                    gated_last[0] = c_ve.n

                def t_gT(t, s=s, bi=bi, gd=gated_done,
                         ptv=prev_T_evac.get((s, 14 + bi % 2), 0),
                         pge=prev_gTe.get((s, bi - 2), 0)):
                    t.wait_ge(c_ve.h, gd)
                    if bi < 2 and ptv:
                        t.wait_ge(c_ve.h, ptv)
                    if pge:
                        t.wait_ge(c_ve.h, pge)
                    bank = pp[6 + bi % 2].ap().bitcast(BF16)
                    last = None
                    for c in range(NCH):
                        last = t.transpose(bank[:, c * 128:(c + 1) * 128],
                                           gtm(bi)[:, c * 128:(c + 1) * 128], idn_bf[:])
                    c_te.inc(last)
                block.tensor(t_gT)
                gT_done_l[bi] = c_te.n

                if bi > 0:
                    mk_ao1(bi - 1)

                def v_gTe(vec, bi=bi, gd=gT_done_l[bi]):
                    vec.wait_ge(c_te.h, gd)
                    src2 = pp[6 + bi % 2].ap().bitcast(BF16)
                    dst = gfm()[:].rearrange("p (c n) -> p c n", c=NCH)[
                        :, :, bi * 128:(bi + 1) * 128]
                    c_ve.inc(vec.tensor_copy(dst, src2[:].rearrange("p (c n) -> p c n", c=NCH)))
                block.vector(v_gTe)
                prev_gTe[(s, bi)] = c_ve.n
                gTe_done = c_ve.n

                def t_wo(t, s=s, bi=bi, ge=gTe_done, vh=vh_l[3],
                         pa=ao1_done_l.get((s, bi - 1)) or (ao1_done_l.get((s - 1, 3), 0) if bi == 0 else 0)):
                    t.wait_ge(c_ve.h, ge)
                    if s == 0 and bi == 0:
                        t.wait_ge(s_wl.h, WO_LOADED)
                        t.wait_ge(c_se.h, Q_DONE)
                    if bi == 0:
                        t.wait_ge(c_ve.h, vh)
                    if pa:
                        t.wait_ge(c_ve.h, pa)
                    gc = gfm()[:].rearrange("p (c n) -> p c n", c=NCH)
                    last = None
                    for c in range(NCH):
                        lhs = gc[:, c, bi * 128:(bi + 1) * 128]
                        t.matmul(pp[2][:, :], lhs, wo_bf(c)[:, 0:512],
                                 start=(c == 0), stop=(c == NCH - 1))
                        last = t.matmul(pp[3][:, :], lhs, wo_bf(c)[:, 512:1024],
                                        start=(c == 0), stop=(c == NCH - 1))
                    c_te.inc(last)
                block.tensor(t_wo)
                wo_done_l[bi] = c_te.n
                if bi > 0:
                    mk_r2m(bi - 1)
            mk_ao1(3)
            mk_r2m(3)

            def vE(vec, lss=lss_box[0]):
                vec.wait_ge(c_se.h, lss)
                vec.drain()
                vec.tensor_scalar(sp1[:, 40:44], sp1[:, 36:40], 1.0 / HID, None, ALU.mult)
                vec.tensor_scalar(sp1[:, 32:36], sp1[:, 32:36], 1.0 / HID, None, ALU.mult)
                vec.drain()
                vec.tensor_tensor(sp1[:, 44:48], sp1[:, 32:36], sp1[:, 32:36], ALU.mult)
                vec.drain()
                c_ve.inc(vec.scalar_tensor_tensor(sp1[:, 40:44], sp1[:, 40:44], EPS,
                                                  sp1[:, 44:48], ALU.add, ALU.subtract))
            block.vector(vE)
            LVAR_DONE = c_ve.n
            SD_TGT = s + 1

            def sD(sc, lv=LVAR_DONE):
                sc.wait_ge(c_ve.h, lv)
                sc.activation(sp1[:, 44:48], sp1[:, 40:44], ACT.Sqrt,
                              bias=0.0, scale=1.0).then_inc(s_sd.h, 1)

            def make_sD():
                s_sd.reserve(1)
                dsc.add(sD)

            def vF(vec, sdt=SD_TGT):
                vec.wait_ge(s_sd.h, sdt)
                vec.reciprocal(sp1[:, 48:52], sp1[:, 44:48])
                vec.drain()
                c_ve.inc(vec.scalar_tensor_tensor(sp1[:, 52:56], sp1[:, 32:36], -1.0,
                                                  sp1[:, 48:52], ALU.mult, ALU.mult))
            block.vector(vF)
            LSTATS = c_ve.n

            a2_l = {}
            lnT_l = {}

            def emit_a2(bi, s=s):
                b = blks[bi]

                def v_apply2(vec, b=b, bi=bi,
                             gate=(gT_done_l.get(bi + 2) if bi < 2 else lnT_l.get(bi - 2, 0))):
                    if gate:
                        vec.wait_ge(c_te.h, gate)
                    c_ve.inc(vec.tensor_scalar(gtm(bi)[:, :], x_tm[:, b * HID:(b + 1) * HID],
                                               sp1[:, 48 + bi:49 + bi],
                                               sp1[:, 52 + bi:53 + bi], ALU.mult, ALU.add))
                block.vector(v_apply2)
                a2_l[bi] = c_ve.n

            def emit_lnT(bi, s=s):
                def t_ln2T(t, s=s, bi=bi, ad=a2_l[bi], hv=prev_h_evac,
                           pv=prev_lnTe.get((s, bi - 1), 0)):
                    t.wait_ge(c_ve.h, ad)
                    if bi == 0:
                        t.wait_ge(c_ve.h, hv)
                    if pv:
                        t.wait_ge(c_ve.h, pv)
                    bank = pp[1].ap().bitcast(BF16)
                    last = None
                    for c in range(NCH):
                        last = t.transpose(bank[:, c * 128:(c + 1) * 128],
                                           gtm(bi)[:, c * 128:(c + 1) * 128], idn_bf[:])
                    c_te.inc(last)
                block.tensor(t_ln2T)
                lnT_l[bi] = c_te.n

            def emit_lne(bi, s=s):
                b = blks[bi]

                def v_ln2e(vec, b=b, bi=bi, td=lnT_l[bi]):
                    vec.wait_ge(c_te.h, td)
                    src2 = pp[1].ap().bitcast(BF16)
                    dst = lnc_view[:, :, b * 128:(b + 1) * 128]
                    c_ve.inc(vec.tensor_copy(dst, src2[:].rearrange("p (c n) -> p c n", c=NCH)))
                block.vector(v_ln2e)
                prev_lnTe[(s, bi)] = c_ve.n
                ve_ln2fm[b] = c_ve.n

            emit_a2(0)
            emit_a2(1)
            if s + 1 < NSTRIP:
                emit_num_phase(s + 1, inject={
                    1: [make_sD],
                    4: [lambda: emit_lnT(0), lambda: emit_a2(2)],
                    6: [lambda: emit_lne(0)],
                    8: [lambda: emit_lnT(1), lambda: emit_a2(3)],
                    10: [lambda: emit_lne(1)],
                    12: [lambda: emit_lnT(2)],
                    13: [lambda: emit_lne(2)],
                    14: [lambda: emit_lnT(3)]})
                emit_lne(3)
            else:
                make_sD()
                emit_lnT(0)
                emit_a2(2)
                emit_lne(0)
                emit_lnT(1)
                emit_a2(3)
                emit_lne(1)
                emit_lnT(2)
                emit_lne(2)
                emit_lnT(3)
                emit_lne(3)

        STRIPS_TE = c_te.n
        STRIPS_VE = c_ve.n
        STRIPS_SE = c_se.n

        prev_eT_te = 0
        prev_p7_ve = 0
        for s in range(NSTRIP):
            def t_rt(t, s=s, le=ve_ln2fm[NBLK - 1], ph=prev_h_evac,
                     st3=prev_num_evac[(NSTRIP - 1, HEADS - 1)]):
                if s == 0:
                    t.wait_ge(c_ve.h, le)
                    t.wait_ge(c_ve.h, ph)
                    t.wait_ge(c_se.h, st3)
                last = None
                for c in range(NCH):
                    last = t.matmul(pp[4][0:8, :], wrt_bf[:, c * 8:(c + 1) * 8],
                                    lnc_view[:, c, s * 512:(s + 1) * 512],
                                    start=(c == 0), stop=(c == NCH - 1))
                c_te.inc(last)
            block.tensor(t_rt)
            rt_done = c_te.n

            exp_done = c_se.reserve(1)

            def s_exp(sc, rd=rt_done, pe=prev_eT_te, lt=ve_ln2fm[NBLK - 1]):
                sc.wait_ge(c_te.h, rd)
                if pe:
                    sc.wait_ge(c_te.h, pe)
                else:
                    sc.wait_ge(c_ve.h, lt)
                sc.activation(e_sb()[:, :], pp[4][0:8, :], ACT.Exp,
                              bias=0.0, scale=1.0).then_inc(c_se.h, 1)
            dsc.add(s_exp)

            def t_eT(t, ed=exp_done, pv=prev_p7_ve):
                t.wait_ge(c_se.h, ed)
                if pv:
                    t.wait_ge(c_ve.h, pv)
                last = None
                for b4 in range(4):
                    last = t.transpose(pp[5][:, b4 * 8:b4 * 8 + 8],
                                       e_sb()[0:8, b4 * 128:(b4 + 1) * 128], idn[0:8, 0:8])
                c_te.inc(last)
            block.tensor(t_eT)
            eT_done = c_te.n
            prev_eT_te = c_te.n

            def v_p7(vec, s=s, ed=eT_done):
                vec.wait_ge(c_te.h, ed)
                vec.tensor_copy(eT_sb[:, 0:32], pp[5][:, 0:32])
                vec.drain()
                for b4 in range(4):
                    vec.reduce_sum(stat[:, S_T0:S_T0 + 1], eT_sb[:, b4 * 8:(b4 + 1) * 8],
                                   axis=AX)
                    vec.drain()
                    vec.reciprocal(stat[:, S_T1:S_T1 + 1], stat[:, S_T0:S_T0 + 1])
                    vec.drain()
                    vec.tensor_scalar(p7_tm[:, s * 4 + b4:s * 4 + b4 + 1],
                                      eT_sb[:, b4 * 8 + 7:b4 * 8 + 8],
                                      stat[:, S_T1:S_T1 + 1], None, ALU.mult)
                c_ve.inc(vec.drain())
            block.vector(v_p7)
            prev_p7_ve = c_ve.n
        P7_DONE = c_ve.n
        dsc.emit(block)
        if DEBUG:
            def dbg_r2(sync):
                sync.wait_ge(c_ve.h, ve_r2[NBLK - 1])
                s_dbg.inc(sync.dma_start(out=dbg["d_r2"].ap(), in_=x_tm[:]), 16)
                sync.wait_ge(c_ve.h, P7_DONE)
                s_dbg.inc(sync.dma_start(out=dbg["d_p7"].ap(), in_=p7_tm[:]), 16)
            block.sync(dbg_r2)
        DBG_END2A = s_dbg.n
        A2A_TE_DONE = c_te.n
        A2A_VE_DONE = c_ve.n
        A2A_SE_DONE = c_se.n

        NFS = 2
        te_w1_consumed = []
        te_w2_consumed = []
        se_gelu = {}
        ve_yev = {}
        ve_out = [0] * NBLK
        WL_BASE = s_wl.n

        def w1_count(S, m):
            return WL_BASE + 16 * (S * (NMCH + NCH) + m + 1)

        def w2_count(S, j):
            return WL_BASE + 16 * (S * (NMCH + NCH) + NMCH + j + 1)

        for S in range(NFS):
            for m in range(NMCH):
                def t_u(t, S=S, m=m):
                    if S == 0 and m == 0:
                        t.wait_ge(c_ve.h, STRIPS_VE)
                        t.wait_ge(c_se.h, STRIPS_SE)
                    if S > 0 and m == 0:
                        t.wait_ge(c_ve.h, ve_out[S * 8 - 1])
                    t.wait_ge(s_wl.h, w1_count(S, m))
                    key = m - 2
                    if key >= 0:
                        t.wait_ge(c_se.h, se_gelu[(S, key)])
                    elif S > 0:
                        t.wait_ge(c_se.h, se_gelu[(S - 1, key + NMCH)])
                    pk = (pp[0], pp[1]) if m % 2 == 0 else (pp[2], pp[3])
                    last = None
                    for c in range(NCH):
                        lhsT = w1r(m)[:, c * 128:(c + 1) * 128]
                        t.matmul(pk[0][:, :], lhsT,
                                 lnc_view[:, c, S * 1024:S * 1024 + 512],
                                 start=(c == 0), stop=(c == NCH - 1))
                        last = t.matmul(pk[1][:, :], lhsT,
                                        lnc_view[:, c, S * 1024 + 512:S * 1024 + 1024],
                                        start=(c == 0), stop=(c == NCH - 1))
                    c_te.inc(last)
                block.tensor(t_u)
                te_w1_consumed.append(c_te.n)
                u_done = c_te.n
                se_gelu[(S, m)] = c_se.reserve(1)

                def s_gelu(sc, m=m, ud=u_done, dq=(DBG_END2A if DEBUG else 0), S=S):
                    sc.wait_ge(c_te.h, ud)
                    if DEBUG and m == 0 and S == 0:
                        sc.wait_ge(s_dbg.h, dq)
                    pk = (pp[0], pp[1]) if m % 2 == 0 else (pp[2], pp[3])
                    sc.activation(ug(m)[:, 0:512], pk[0][:, :], ACT.Gelu, bias=0.0, scale=1.0)
                    sc.activation(ug(m)[:, 512:1024], pk[1][:, :], ACT.Gelu,
                                  bias=0.0, scale=1.0).then_inc(c_se.h, 1)
                dsc.add(s_gelu)
            GELU_ALL = se_gelu[(S, NMCH - 1)]

            for j in range(NCH):
                def t_y(t, S=S, j=j, ga=GELU_ALL):
                    t.wait_ge(s_wl.h, w2_count(S, j))
                    t.wait_ge(c_se.h, ga)
                    if S == 0 and j == 0:
                        t.wait_ge(c_ve.h, P7_DONE)
                    key = j - 2
                    if key >= 0:
                        t.wait_ge(c_ve.h, ve_yev[(S, key)])
                    elif S > 0:
                        t.wait_ge(c_ve.h, ve_yev[(S - 1, key + NCH)])
                    pk = (pp[4], pp[5]) if j % 2 == 0 else (pp[6], pp[7])
                    last = None
                    for m in range(NMCH):
                        lhsT = w2r(j)[:, m * 128:(m + 1) * 128]
                        t.matmul(pk[0][:, :], lhsT, ug(m)[:, 0:512],
                                 start=(m == 0), stop=(m == NMCH - 1))
                        last = t.matmul(pk[1][:, :], lhsT, ug(m)[:, 512:1024],
                                        start=(m == 0), stop=(m == NMCH - 1))
                    c_te.inc(last)
                block.tensor(t_y)
                te_w2_consumed.append(c_te.n)
                y_done = c_te.n

                def v_yev(vec, j=j, yd=y_done):
                    vec.wait_ge(c_te.h, yd)
                    pk = (pp[4], pp[5]) if j % 2 == 0 else (pp[6], pp[7])
                    vec.tensor_copy(yfm(j)[:, 0:512], pk[0][:, :])
                    c_ve.inc(vec.tensor_copy(yfm(j)[:, 512:1024], pk[1][:, :]))
                block.vector(v_yev)
                ve_yev[(S, j)] = c_ve.n
            Y_ALL = ve_yev[(S, NCH - 1)]
            if DEBUG and S == 0:
                def dbg_y(sync, ya=Y_ALL):
                    sync.wait_ge(c_ve.h, ya)
                    s_dbg.inc(sync.dma_start(out=dbg["d_y"].ap(),
                                             in_=ap_bf16(A_YFM, 128, NCH * 1024)), 16)
                block.sync(dbg_y)
            DBG_Y = s_dbg.n

            for bi in range(8):
                gblk = S * 8 + bi

                def t_yT(t, S=S, bi=bi, ya=Y_ALL, gblk=gblk, ga=GELU_ALL):
                    t.wait_ge(c_ve.h, ya)
                    t.wait_ge(c_se.h, ga)
                    if gblk >= 2:
                        t.wait_ge(c_ve.h, ve_out[gblk - 2])
                    bank = pp[gblk % 2].ap().bitcast(BF16)
                    last = None
                    for j in range(NCH):
                        last = t.transpose(bank[:, j * 128:(j + 1) * 128],
                                           yfm(j)[:, bi * 128:(bi + 1) * 128], idn_bf[:])
                    c_te.inc(last)
                block.tensor(t_yT)
                yT_done = c_te.n

                def v_out(vec, gblk=gblk, yd=yT_done, dgy=(DBG_Y if DEBUG else 0)):
                    vec.wait_ge(c_te.h, yd)
                    vec.wait_ge(c_ve.h, P7_DONE)
                    if DEBUG:
                        vec.wait_ge(s_dbg.h, dgy)
                    bank = pp[gblk % 2].ap().bitcast(BF16)
                    xb = x_tm[:, gblk * HID:(gblk + 1) * HID]
                    c_ve.inc(vec.scalar_tensor_tensor(
                        xb, bank[:, 0:1024], p7_tm[:, gblk:gblk + 1], xb, ALU.mult, ALU.add))
                block.vector(v_out)
                ve_out[gblk] = c_ve.n

            def sy_store(sync, S=S, vo1=ve_out[S * 8 + 3], vo2=ve_out[S * 8 + 7]):
                orr = out_d.ap().rearrange("(t p) k -> p t k", p=128)
                xrr = x_tm[:].rearrange("p (t k) -> p t k", t=NBLK)
                sync.wait_ge(c_ve.h, vo1)
                s_st.inc(sync.dma_start(out=orr[:, S * 8:S * 8 + 4, :],
                                        in_=xrr[:, S * 8:S * 8 + 4, :]), 16)
                sync.wait_ge(c_ve.h, vo2)
                s_st.inc(sync.dma_start(out=orr[:, S * 8 + 4:S * 8 + 8, :],
                                        in_=xrr[:, S * 8 + 4:S * 8 + 8, :]), 16)
            block.sync(sy_store)

        dsc.emit(block)

        def gp_ffn(gp):
            gp.wait_ge(c_te.h, STRIPS_TE)
            gp.wait_ge(c_ve.h, STRIPS_VE)
            gp.wait_ge(c_se.h, STRIPS_SE)
            n = 0
            for S in range(NFS):
                for m in range(NMCH):
                    idx = S * NMCH + m
                    if idx >= 4:
                        gp.wait_ge(c_te.h, te_w1_consumed[idx - 4])
                    gp.dma_start(
                        out=w1r(m).rearrange("p (c j) -> p c j", c=NCH),
                        in_=w1_d.ap()[:, m * 128:(m + 1) * 128].rearrange(
                            "(c k) j -> k c j", c=NCH)).then_inc(s_wl.h, 16)
                    n += 16
                for j in range(NCH):
                    idx = S * NCH + j
                    if idx >= 2:
                        gp.wait_ge(c_te.h, te_w2_consumed[idx - 2])
                    gp.dma_start(
                        out=w2r(j).rearrange("p (mc jj) -> p mc jj", mc=NMCH),
                        in_=w2_d.ap()[:, j * 128:(j + 1) * 128].rearrange(
                            "(mc k) jj -> k mc jj", mc=NMCH)).then_inc(s_wl.h, 16)
                    n += 16
            s_wl.n += 0
        block.gpsimd(gp_ffn)

        def sy_final(sync):
            sync.wait_ge(s_st.h, s_st.n)
        block.sync(sy_final)

    return nc


def _permutation():
    perm = np.empty(HID, dtype=np.int64)
    for fp in range(HID):
        h, d = fp // DH, fp % DH
        perm[fp] = d * HEADS + h
    return perm


def kernel(L0_attn_x, L0_attn_Wh, L0_attn_bh, L0_attn_Wq, L0_attn_bq, L0_attn_Wk,
           L0_attn_bk, L0_attn_Wv, L0_attn_bv, L0_attn_beta, L0_attn_ln_w,
           L0_attn_ln_b, L0_attn_Wo, L0_attn_bo, L0_ffn_Wr, L0_ffn_W1, L0_ffn_b1,
           L0_ffn_W2, L0_ffn_b2):
    perm = _permutation()
    x = np.ascontiguousarray(np.asarray(L0_attn_x, dtype=np.float32).reshape(SEQ, HID))
    wq = np.ascontiguousarray(np.asarray(L0_attn_Wq, np.float32)[:, perm])
    wk = np.ascontiguousarray(np.asarray(L0_attn_Wk, np.float32)[:, perm])
    wv = np.ascontiguousarray(np.asarray(L0_attn_Wv, np.float32)[:, perm])
    wh = np.ascontiguousarray(np.asarray(L0_attn_Wh, np.float32)[:, perm])
    wo = np.ascontiguousarray(np.asarray(L0_attn_Wo, np.float32)[perm, :])
    bh = np.ascontiguousarray(np.asarray(L0_attn_bh, np.float32)[perm].reshape(1, HID))
    beta = np.ascontiguousarray(np.asarray(L0_attn_beta, np.float32)[perm].reshape(1, HID))
    bo = np.ascontiguousarray(np.asarray(L0_attn_bo, np.float32).reshape(1, HID))
    wrt = np.ascontiguousarray(np.asarray(L0_ffn_Wr, np.float32).T)
    w1 = np.ascontiguousarray(np.asarray(L0_ffn_W1, np.float32)[7])
    w2 = np.ascontiguousarray(np.asarray(L0_ffn_W2, np.float32)[7])
    idn = np.eye(128, dtype=np.float32)

    nc = build_nc()
    in_maps = []
    for c in range(NCORES):
        in_maps.append({
            "x": np.ascontiguousarray(x[c * T:(c + 1) * T]),
            "wq": wq, "wk": wk, "wv": wv, "wh": wh, "wo": wo,
            "bh": bh, "beta": beta, "bo": bo, "wrt": wrt,
            "w1": w1, "w2": w2, "idn_h": idn,
        })
    res = run_bass_kernel_spmd(nc, in_maps, core_ids=list(range(NCORES)))
    if DEBUG:
        global DBG_RESULTS
        DBG_RESULTS = res.results
    out = np.concatenate([res.results[c]["out"] for c in range(NCORES)], axis=0)
    return out.reshape(1, SEQ, HID).astype(np.float32)


# version 24
# speedup vs baseline: 1.3663x; 1.0031x over previous
import contextlib
import numpy as np
import concourse.bass as bass
import concourse.mybir as mybir
from concourse.bass_utils import run_bass_kernel_spmd

F32 = mybir.dt.float32
BF16 = mybir.dt.bfloat16
AX = mybir.AxisListType.X
AXY = mybir.AxisListType.XY
ALU = mybir.AluOpType
ACT = mybir.ActivationFunctionType

NCORES = 8
HID = 1024
HEADS = 16
DH = 64
SEQ = 16384
T = SEQ // NCORES
NBLK = T // 128
NCH = HID // 128
EPS = 1e-5
FF = 4096
NMCH = FF // 128

A_QBF = 0
A_WK = 0
A_WV = 16384
A_KTM = 32768
A_HPC = 32768
A_VTM = 36864
A_ATM = 36864
A_LN1TM = 40960
A_KVF = 40960
A_WQ = 45184
A_GFM = 53504
A_NUM = 61696
A_CBB = 65792
A_BOB = 67840
A_KVBF = 69888
A_WH = 72000
A_WO = 88384
A_UG = 0
A_YFM = 65536
A_W1R = 81920
A_W2R = 90112
A_BYTES = 106496
DEBUG = False


class Ctr:
    def __init__(self, es, nc, name):
        self.h = es.enter_context(nc.semaphore(name))
        self.n = 0

    def inc(self, instr, k=1):
        instr.then_inc(self.h, k)
        self.n += k
        return self.n

    def reserve(self, k=1):
        self.n += k
        return self.n


class DeferredScalar:

    def __init__(self):
        self.fns = []

    def add(self, fn):
        self.fns.append(fn)

    def emit(self, block):
        fns, self.fns = self.fns, []
        if fns:
            def body(sc):
                for f in fns:
                    f(sc)
            block.scalar(body)


def build_nc():
    nc = bass.Bass("TRN2", target_bir_lowering=False, debug=False)

    x_d = nc.dram_tensor("x", [T, HID], F32, kind="ExternalInput")
    wq_d = nc.dram_tensor("wq", [HID, HID], F32, kind="ExternalInput")
    wk_d = nc.dram_tensor("wk", [HID, HID], F32, kind="ExternalInput")
    wv_d = nc.dram_tensor("wv", [HID, HID], F32, kind="ExternalInput")
    wh_d = nc.dram_tensor("wh", [HID, HID], F32, kind="ExternalInput")
    wo_d = nc.dram_tensor("wo", [HID, HID], F32, kind="ExternalInput")
    bh_d = nc.dram_tensor("bh", [1, HID], F32, kind="ExternalInput")
    beta_d = nc.dram_tensor("beta", [1, HID], F32, kind="ExternalInput")
    bo_d = nc.dram_tensor("bo", [1, HID], F32, kind="ExternalInput")
    wrt_d = nc.dram_tensor("wrt", [HID, 8], F32, kind="ExternalInput")
    w1_d = nc.dram_tensor("w1", [HID, FF], F32, kind="ExternalInput")
    w2_d = nc.dram_tensor("w2", [FF, HID], F32, kind="ExternalInput")
    idn_d = nc.dram_tensor("idn_h", [128, 128], F32, kind="ExternalInput")
    out_d = nc.dram_tensor("out", [T, HID], F32, kind="ExternalOutput")
    dbg = {}
    if DEBUG:
        dbg["d_kv"] = nc.dram_tensor("d_kv", [64, HEADS * 65], F32, kind="ExternalOutput")
        dbg["d_kvbf"] = nc.dram_tensor("d_kvbf", [128, HEADS * 65], BF16, kind="ExternalOutput")
        dbg["d_ln1"] = nc.dram_tensor("d_ln1", [128, NCH * T], BF16, kind="ExternalOutput")
        dbg["d_q"] = nc.dram_tensor("d_q", [128, NCH * T], BF16, kind="ExternalOutput")
        dbg["d_attn0"] = nc.dram_tensor("d_attn0", [128, 1040], F32, kind="ExternalOutput")
        dbg["d_r2"] = nc.dram_tensor("d_r2", [128, NBLK * HID], F32, kind="ExternalOutput")
        dbg["d_p7"] = nc.dram_tensor("d_p7", [128, NBLK], F32, kind="ExternalOutput")
        dbg["d_y"] = nc.dram_tensor("d_y", [128, NCH * 1024], BF16, kind="ExternalOutput")
        dbg["d_hpc0"] = nc.dram_tensor("d_hpc0", [128, 1024], BF16, kind="ExternalOutput")
        dbg["d_q2"] = nc.dram_tensor("d_q2", [128, 2048], BF16, kind="ExternalOutput")
        dbg["d_cb"] = nc.dram_tensor("d_cb", [128, 1024], BF16, kind="ExternalOutput")
        dbg["d_g"] = nc.dram_tensor("d_g", [128, 4096], BF16, kind="ExternalOutput")
        dbg["d_ao0"] = nc.dram_tensor("d_ao0", [128, 1024], F32, kind="ExternalOutput")
        dbg["d_gtm0"] = nc.dram_tensor("d_gtm0", [128, 1024], BF16, kind="ExternalOutput")
        dbg["d_lnat0"] = nc.dram_tensor("d_lnat0", [128, 1040], F32, kind="ExternalOutput")
        dbg["d_stat0"] = nc.dram_tensor("d_stat0", [128, 8], F32, kind="ExternalOutput")
        dbg["d_wq0"] = nc.dram_tensor("d_wq0", [128, 1024], BF16, kind="ExternalOutput")
    cc_in = nc.dram_tensor("cc_in", [64, HEADS * 65], BF16, kind="Internal")
    cc_out = nc.dram_tensor("cc_out", [64, HEADS * 65], BF16, kind="Internal")

    es = contextlib.ExitStack()
    x_tm = es.enter_context(nc.sbuf_tensor("x_tm", [128, NBLK * HID], F32))
    lnx_fm = es.enter_context(nc.sbuf_tensor("lnx_fm", [128, NCH * T], BF16))
    arena = es.enter_context(nc.sbuf_tensor("arena", [128, A_BYTES // 4], F32))
    idn = es.enter_context(nc.sbuf_tensor("idnf", [128, 128], F32))
    idn_bf = es.enter_context(nc.sbuf_tensor("idnb", [128, 128], BF16))
    wrt_bf = es.enter_context(nc.sbuf_tensor("wrtb", [128, 64], BF16))
    scr2 = es.enter_context(nc.sbuf_tensor("scr2", [128, 512], F32))
    eT_sb = es.enter_context(nc.sbuf_tensor("eT_sb", [128, 48], F32))
    p7_tm = es.enter_context(nc.sbuf_tensor("p7_tm", [128, NBLK], F32))
    sq_scr = es.enter_context(nc.sbuf_tensor("sq_scr", [128, HID], BF16))
    stat = es.enter_context(nc.sbuf_tensor("stat", [128, 8], F32))
    sp1 = es.enter_context(nc.sbuf_tensor("sp1", [128, 96], F32))
    ones_bf = es.enter_context(nc.sbuf_tensor("ones_bf", [128, 1], BF16))
    ones_row = es.enter_context(nc.sbuf_tensor("ones_row", [1, 128], BF16))
    eps_t = es.enter_context(nc.sbuf_tensor("eps_t", [128, 1], F32))
    nh_t = es.enter_context(nc.sbuf_tensor("nh_t", [128, 1], F32))

    S_MU, S_SS, S_VAR, S_S, S_NEGMS, S_T0, S_T1 = 0, 1, 2, 3, 4, 5, 6

    pp = [es.enter_context(nc.psum_tensor(f"pp{i}", [128, 512], F32)) for i in range(8)]

    def ap_f32(off, rows, cols):
        return arena[0:rows, off // 4: off // 4 + cols]

    def ap_bf16(off, rows, cols):
        return arena[:, off // 4: off // 4 + (cols + 1) // 2].bitcast(BF16)[0:rows, 0:cols]

    def wk_bf(c):
        return ap_bf16(A_WK + c * 2048, 128, 1024)

    def wv_bf(c):
        return ap_bf16(A_WV + c * 2048, 128, 1024)

    def wh_bf(c):
        return ap_bf16(A_WH + c * 2048, 128, 1024)

    def wq_bf(c):
        return ap_bf16(A_WQ + c * 2048, 128, 1024)

    def wo_bf(c):
        return ap_bf16(A_WO + c * 2048, 128, 1024)

    def ktm(i):
        return ap_bf16(A_KTM + (i % 2) * 2048, 128, 1024)

    def vtm(i):
        return ap_bf16(A_VTM + (i % 2) * 2048, 128, 1024)

    def ln1tm(i):
        return ap_bf16(A_LN1TM + (i % 2) * 2048, 128, 1024)

    def qbf():
        return ap_bf16(A_QBF, 128, 16384)

    _HPC = [A_HPC, A_HPC + 2048, A_NUM, A_NUM + 2048]

    def hpc(b):
        return ap_bf16(_HPC[b % 4], 128, 1024)

    def atm():
        return ap_f32(A_ATM, 128, 4160)

    def gfm():
        return ap_bf16(A_GFM, 128, 4096)

    def numsl(j):
        return ap_f32(A_NUM + (j % 2) * 2048, 128, 512)

    def gtm(b):
        if b % 2 == 0:
            return sq_scr[:, 0:1024]
        return scr2[:].bitcast(BF16)[0:128, 0:1024]

    def e_sb():
        return scr2[0:8, :]

    cb_b = ap_bf16(A_CBB, 128, 1024)
    bo_b = ap_bf16(A_BOB, 128, 1024)
    kvbf = ap_bf16(A_KVBF, 128, 1040)
    kvf32 = ap_bf16(A_KVF, 64, 1040)
    bias_t1 = ap_f32(A_ATM, 128, 1024)
    bias_t2 = ap_f32(A_ATM + 4096, 128, 1024)

    def ug(m):
        return ap_bf16(A_UG + m * 2048, 128, 1024)

    def yfm(j):
        return ap_bf16(A_YFM + j * 2048, 128, 1024)

    def w1r(m):
        return ap_bf16(A_W1R + (m % 4) * 2048, 128, 1024)

    def w2r(j):
        return ap_bf16(A_W2R + (j % 2) * 8192, 128, 4096)

    rc_sb = es.enter_context(nc.sbuf_tensor("rc_sb", [128, 64], F32))
    c_te = Ctr(es, nc, "c_te")
    c_ve = Ctr(es, nc, "c_ve")
    c_se = Ctr(es, nc, "c_se")
    s_sd = Ctr(es, nc, "s_sd")
    s_ls = Ctr(es, nc, "s_ls")
    s_x = Ctr(es, nc, "s_x")
    s_st = Ctr(es, nc, "s_st")
    s_wl = Ctr(es, nc, "s_wl")
    s_sq = Ctr(es, nc, "s_sq")
    s_cc = Ctr(es, nc, "s_cc")
    s_dbg = Ctr(es, nc, "s_dbg")

    lnc = None

    with nc.Block() as block:
        lnc_view = lnx_fm[:].rearrange("p (c n) -> p c n", c=NCH)

        X_Q = []

        def init_sync(sync):
            xr = x_d.ap().rearrange("(t p) k -> p t k", p=128)
            xo = x_tm[:].rearrange("p (t k) -> p t k", t=NBLK)
            for qtr in range(4):
                s_x.inc(sync.dma_start(out=xo[:, qtr * 4:(qtr + 1) * 4, :],
                                       in_=xr[:, qtr * 4:(qtr + 1) * 4, :]), 16)
                X_Q.append(s_x.n)
            s_x.inc(sync.dma_start(out=idn[:], in_=idn_d.ap()), 16)
        block.sync(init_sync)
        X_LOADED = s_x.n

        def init_gp(gp):
            s_wl.inc(gp.dma_start(out=bias_t1, in_=bh_d.ap().partition_broadcast(128)), 16)
            s_wl.inc(gp.dma_start(out=bias_t2, in_=beta_d.ap().partition_broadcast(128)), 16)
            s_wl.inc(gp.dma_start(out=bo_b, in_=bo_d.ap().partition_broadcast(128)), 16)
            s_wl.inc(gp.dma_start(
                out=ap_bf16(A_WK, 128, NCH * 1024).rearrange("p (c m) -> p c m", c=NCH),
                in_=wk_d.ap().rearrange("(c k) m -> k c m", c=NCH)), 16)
            s_wl.inc(gp.dma_start(
                out=ap_bf16(A_WV, 128, NCH * 1024).rearrange("p (c m) -> p c m", c=NCH),
                in_=wv_d.ap().rearrange("(c k) m -> k c m", c=NCH)), 16)
            s_wl.inc(gp.dma_start(
                out=ap_bf16(A_WH, 128, NCH * 1024).rearrange("p (c m) -> p c m", c=NCH),
                in_=wh_d.ap().rearrange("(c k) m -> k c m", c=NCH)), 16)
            gp.wait_ge(s_x.h, X_LOADED)
            s_wl.inc(gp.dma_start(out=idn_bf[:], in_=idn[:]), 16)
            s_wl.inc(gp.dma_start(
                out=wrt_bf[:].rearrange("p (c e) -> p c e", c=NCH),
                in_=wrt_d.ap().rearrange("(c k) e -> k c e", c=NCH)), 16)
        block.gpsimd(init_gp)
        BIAS_LOADED = 32
        P1W_LOADED = s_wl.n

        def init_vec(vec):
            vec.memset(ones_bf[:], 1.0)
            vec.memset(ones_row[:], 1.0)
            vec.memset(eps_t[:], EPS)
            vec.memset(nh_t[:], -0.5)
            vec.wait_ge(s_wl.h, P1W_LOADED)
            c_ve.inc(vec.tensor_tensor(cb_b, bias_t1, bias_t2, ALU.add))
        block.vector(init_vec)
        CB_DONE = c_ve.n

        def p1_pre_v(vec):
            vec.wait_ge(s_x.h, X_LOADED)
        block.vector(p1_pre_v)

        def p1_pre_s(sc):
            sc.wait_ge(s_x.h, X_LOADED)
        block.scalar(p1_pre_s)

        def p1_stats_v(vec):
            for i in range(NBLK):
                if i % 4 == 0:
                    vec.wait_ge(s_x.h, X_Q[i // 4])
                vec.reduce_sum(sp1[:, i:i + 1], x_tm[:, i * HID:(i + 1) * HID], axis=AX)
        block.vector(p1_stats_v)

        def p1_stats_s(sc):
            last = None
            for i in range(NBLK):
                if i % 4 == 0:
                    sc.wait_ge(s_x.h, X_Q[i // 4])
                last = sc.activation(sq_scr[:, 0:1024], x_tm[:, i * HID:(i + 1) * HID],
                                     ACT.Square, bias=0.0, scale=1.0,
                                     accum_out=sp1[:, 16 + i:17 + i])
            c_se.inc(last)
        block.scalar(p1_stats_s)
        P1_SS = c_se.n

        def p1_chain_v(vec):
            vec.wait_ge(c_se.h, P1_SS)
            vec.drain()
            vec.tensor_scalar(sp1[:, 32:48], sp1[:, 16:32], 1.0 / HID, None, ALU.mult)
            vec.tensor_scalar(sp1[:, 0:16], sp1[:, 0:16], 1.0 / HID, None, ALU.mult)
            vec.drain()
            vec.tensor_tensor(sp1[:, 48:64], sp1[:, 0:16], sp1[:, 0:16], ALU.mult)
            vec.drain()
            c_ve.inc(vec.scalar_tensor_tensor(sp1[:, 48:64], sp1[:, 32:48], EPS,
                                              sp1[:, 48:64], ALU.add, ALU.subtract))
        block.vector(p1_chain_v)
        P1_VAR = c_ve.n

        def p1_chain_s(sc):
            sc.wait_ge(c_ve.h, P1_VAR)
            c_se.inc(sc.activation(sp1[:, 32:48], sp1[:, 48:64], ACT.Sqrt,
                                   bias=0.0, scale=1.0))
        block.scalar(p1_chain_s)
        P1_SA = c_se.n

        def p1_negms_v(vec):
            vec.wait_ge(c_se.h, P1_SA)
            vec.reciprocal(sp1[:, 64:80], sp1[:, 32:48])
            vec.drain()
            c_ve.inc(vec.scalar_tensor_tensor(sp1[:, 80:96], sp1[:, 0:16], -1.0,
                                              sp1[:, 64:80], ALU.mult, ALU.mult))
        block.vector(p1_negms_v)
        P1_NEG = c_ve.n

        te_kv = [0] * NBLK
        se_kevac = [0] * NBLK
        ve_vevac = [0] * NBLK
        ve_lnfm = [0] * NBLK
        lnT_done_list = [0] * NBLK
        dsc = DeferredScalar()

        def p1_apply(i):
            xblk = x_tm[:, i * HID:(i + 1) * HID]

            def v_apply(vec, i=i, xblk=xblk, pT=(lnT_done_list[i - 2] if i >= 2 else 0)):
                if pT:
                    vec.wait_ge(c_te.h, pT)
                c_ve.inc(vec.tensor_scalar(ln1tm(i)[:, :], xblk,
                                           sp1[:, 64 + i:65 + i],
                                           sp1[:, 80 + i:81 + i], ALU.mult, ALU.add))
            block.vector(v_apply)
            return c_ve.n

        def p1_trans(i, ad):
            def t_trans(t, i=i, ad=ad, prev_ev=(ve_lnfm[i - 1] if i > 0 else 0)):
                t.wait_ge(c_ve.h, ad)
                if i > 0:
                    t.wait_ge(c_ve.h, prev_ev)
                dst = pp[7].ap().bitcast(BF16)
                last = None
                for c in range(NCH):
                    last = t.transpose(dst[:, c * 128:(c + 1) * 128],
                                       ln1tm(i)[:, c * 128:(c + 1) * 128], idn_bf[:])
                c_te.inc(last)
            block.tensor(t_trans)
            lnT_done_list[i] = c_te.n
            return c_te.n

        def p1_evac_ln(i, td):
            def v_evac_ln(vec, i=i, td=td):
                vec.wait_ge(c_te.h, td)
                src = pp[7].ap().bitcast(BF16)
                c_ve.inc(vec.tensor_copy(
                    lnc_view[:, :, i * 128:(i + 1) * 128],
                    src[:].rearrange("p (c n) -> p c n", c=NCH)))
            block.vector(v_evac_ln)
            ve_lnfm[i] = c_ve.n

        ad0 = p1_apply(0)
        td0 = p1_trans(0, ad0)
        p1_evac_ln(0, td0)
        for i in range(NBLK):
            def t_kproj(t, i=i, lev=ve_lnfm[i]):
                t.wait_ge(c_ve.h, lev)
                if i == 0:
                    t.wait_ge(s_wl.h, P1W_LOADED)
                if i >= 1:
                    t.wait_ge(c_se.h, se_kevac[i - 1])
                last = None
                for c in range(NCH):
                    lhs = lnc_view[:, c, i * 128:(i + 1) * 128]
                    t.matmul(pp[3][:, :], lhs, wk_bf(c)[:, 0:512],
                             start=(c == 0), stop=(c == NCH - 1))
                    last = t.matmul(pp[4][:, :], lhs, wk_bf(c)[:, 512:1024],
                                    start=(c == 0), stop=(c == NCH - 1))
                c_te.inc(last)
            block.tensor(t_kproj)
            kproj_done = c_te.n

            def t_vproj(t, i=i, pv=(ve_vevac[i - 1] if i >= 1 else 0)):
                if pv:
                    t.wait_ge(c_ve.h, pv)
                last = None
                for c in range(NCH):
                    lhs = lnc_view[:, c, i * 128:(i + 1) * 128]
                    t.matmul(pp[5][:, :], lhs, wv_bf(c)[:, 0:512],
                             start=(c == 0), stop=(c == NCH - 1))
                    last = t.matmul(pp[6][:, :], lhs, wv_bf(c)[:, 512:1024],
                                    start=(c == 0), stop=(c == NCH - 1))
                c_te.inc(last)
            block.tensor(t_vproj)
            vproj_done = c_te.n

            if i + 1 < NBLK:
                ad = p1_apply(i + 1)
                td = p1_trans(i + 1, ad)

            se_kevac[i] = c_se.reserve(1)

            def s_kevac(sc, i=i, kd=kproj_done, tk=(te_kv[i - 2] if i >= 2 else 0)):
                sc.wait_ge(c_te.h, kd)
                if tk:
                    sc.wait_ge(c_te.h, tk)
                sc.activation(ktm(i)[:, 0:512], pp[3][:, :], ACT.Sigmoid, bias=0.0, scale=1.0)
                sc.activation(ktm(i)[:, 512:1024], pp[4][:, :], ACT.Sigmoid,
                              bias=0.0, scale=1.0).then_inc(c_se.h, 1)
            dsc.add(s_kevac)

            def v_vevac(vec, i=i, kd=vproj_done):
                vec.wait_ge(c_te.h, kd)
                if i >= 2:
                    vec.wait_ge(c_te.h, te_kv[i - 2])
                vec.tensor_copy(vtm(i)[:, 0:512], pp[5][:, :])
                c_ve.inc(vec.tensor_copy(vtm(i)[:, 512:1024], pp[6][:, :]))
            block.vector(v_vevac)
            ve_vevac[i] = c_ve.n

            if i + 1 < NBLK:
                p1_evac_ln(i + 1, td)

            def t_kv(t, i=i, ke=se_kevac[i], ve=ve_vevac[i]):
                t.wait_ge(c_se.h, ke)
                t.wait_ge(c_ve.h, ve)
                last = None
                for h in range(HEADS):
                    lhs = ktm(i)[:, h * DH:(h + 1) * DH]
                    bank = pp[0] if h < 8 else pp[1]
                    col = (h % 8) * DH
                    t.matmul(bank[0:DH, col:col + DH], lhs, vtm(i)[:, h * DH:(h + 1) * DH],
                             start=(i == 0 and h % 8 == 0), stop=(i == NBLK - 1),
                             skip_group_check=True)
                    last = t.matmul(pp[2][0:DH, h:h + 1], lhs, ones_bf[:],
                                    start=(i == 0 and h == 0), stop=(i == NBLK - 1),
                                    skip_group_check=True)
                c_te.inc(last)
            block.tensor(t_kv)
            te_kv[i] = c_te.n

        if DEBUG:
            def dbg_ln1(sync):
                sync.wait_ge(c_ve.h, ve_lnfm[NBLK - 1])
                s_dbg.inc(sync.dma_start(out=dbg["d_ln1"].ap(), in_=lnx_fm[:]), 16)
            block.sync(dbg_ln1)
        DBG_LN1 = s_dbg.n

        dsc.emit(block)

        def v_kv_evac(vec):
            vec.wait_ge(c_te.h, te_kv[NBLK - 1])
            kvr = kvf32.rearrange("p (h j) -> p h j", h=HEADS)
            vec.tensor_copy(kvr[:, 0:8, 0:DH],
                            pp[0][0:DH, :].rearrange("p (h j) -> p h j", h=8))
            vec.tensor_copy(kvr[:, 8:16, 0:DH],
                            pp[1][0:DH, :].rearrange("p (h j) -> p h j", h=8))
            c_ve.inc(vec.tensor_copy(kvr[:, :, DH:DH + 1],
                                     pp[2][0:DH, 0:HEADS].unsqueeze(2)))
        block.vector(v_kv_evac)
        KV_EVAC = c_ve.n
        if DEBUG:
            def dbg_kv(gp):
                gp.wait_ge(c_ve.h, KV_EVAC)
                s_dbg.inc(gp.dma_start(out=dbg["d_kv"].ap(), in_=kvf32), 16)
            block.gpsimd(dbg_kv)

        def sc_ccin(sc):
            sc.wait_ge(c_ve.h, KV_EVAC)
            s_sq.inc(sc.dma_start(out=cc_in.ap(), in_=kvf32), 16)
        block.scalar(sc_ccin)
        CCIN = s_sq.n

        def gp_wq(gp):
            s_wl.inc(gp.dma_start(
                out=ap_bf16(A_WQ, 128, NCH * 1024).rearrange("p (c m) -> p c m", c=NCH),
                in_=wq_d.ap().rearrange("(c k) m -> k c m", c=NCH)), 16)
        block.gpsimd(gp_wq)
        WQ_LOADED = s_wl.n

        def gp_cc(gp):
            gp.wait_ge(s_sq.h, CCIN)
            s_cc.inc(gp.collective_compute(
                "AllReduce", ALU.add,
                replica_groups=[list(range(NCORES))],
                ins=[cc_in.ap().opt()],
                outs=[cc_out.ap().opt()]))
            gp.wait_ge(s_cc.h, s_cc.n)
            s_wl.inc(gp.dma_start(out=kvbf[0:64, :], in_=cc_out.ap()), 16)
        block.gpsimd(gp_cc)

        def sy_kvdup(sync):
            sync.wait_ge(s_cc.h, s_cc.n)
            s_wl.inc(sync.dma_start(out=kvbf[64:128, :], in_=cc_out.ap()), 16)
        block.sync(sy_kvdup)

        def gp_wo(gp):
            s_wl.inc(gp.dma_start(
                out=ap_bf16(A_WO, 128, NCH * 1024).rearrange("p (c m) -> p c m", c=NCH),
                in_=wo_d.ap().rearrange("(c k) m -> k c m", c=NCH)), 16)
        block.gpsimd(gp_wo)
        KV_READY = s_wl.n - 16
        if DEBUG:
            def dbg_kvbf(gp, kr=s_wl.n - 16):
                gp.wait_ge(s_wl.h, kr)
                s_dbg.inc(gp.dma_start(out=dbg["d_kvbf"].ap(), in_=kvbf[:, :]), 16)
            block.gpsimd(dbg_kvbf)
        WO_LOADED = s_wl.n

        se_qevac = [0] * NCH
        for m in range(NCH):
            def t_q(t, m=m):
                if m == 0:
                    t.wait_ge(s_wl.h, WQ_LOADED)
                    t.wait_ge(c_ve.h, KV_EVAC)
                    t.wait_ge(c_se.h, se_kevac[NBLK - 1])
                if m >= 2:
                    t.wait_ge(c_se.h, se_qevac[m - 2])
                banks = pp[0:4] if m % 2 == 0 else pp[4:8]
                last = None
                for c in range(NCH):
                    lhsT = wq_bf(c)[:, m * 128:(m + 1) * 128]
                    for seg in range(4):
                        last = t.matmul(banks[seg][:, :], lhsT,
                                        lnc_view[:, c, seg * 512:(seg + 1) * 512],
                                        start=(c == 0), stop=(c == NCH - 1))
                c_te.inc(last)
            block.tensor(t_q)
            q_done_m = c_te.n

            se_qevac[m] = c_se.reserve(1)

            def s_qevac(sc, m=m, qd=q_done_m):
                sc.wait_ge(c_te.h, qd)
                banks = pp[0:4] if m % 2 == 0 else pp[4:8]
                last = None
                for seg in range(4):
                    last = sc.activation(qbf()[:, m * T + seg * 512:m * T + (seg + 1) * 512],
                                         banks[seg][:, :], ACT.Sigmoid, bias=0.0, scale=1.0)
                last.then_inc(c_se.h, 1)
            dsc.add(s_qevac)
            if DEBUG and m == 0:
                def dbg_q2(sync, qe=c_se.n):
                    sync.wait_ge(c_se.h, qe)
                    s_dbg.inc(sync.dma_start(out=dbg["d_q2"].ap(), in_=qbf()[:, 0:2048]), 16)
                    s_dbg.inc(sync.dma_start(out=dbg["d_wq0"].ap(), in_=wq_bf(0)), 16)
                block.sync(dbg_q2)
        Q_DONE = se_qevac[NCH - 1]
        if DEBUG:
            def dbg_q(sync):
                sync.wait_ge(c_se.h, Q_DONE)
                s_dbg.inc(sync.dma_start(out=dbg["d_q"].ap(), in_=qbf()), 16)
            block.sync(dbg_q)
        DBG_Q = s_dbg.n

        NSTRIP = 4
        ve_r2 = [0] * NBLK
        ve_ln2fm = [0] * NBLK
        prev_h_evac = 0
        prev_num_evac = {}
        prev_T_evac = {}
        prev_gTe = {}
        prev_lnTe = {}
        strip_T_done = [0] * NSTRIP
        ao1_done_l = {}
        gated_last = [0]
        dsc2 = DeferredScalar()

        def emit_num_phase(s, inject=None):
            tgt = dsc if inject is not None else dsc2

            def emit_T(h):
                def t_numT(t, s=s, h=h, nev=prev_num_evac[(s, h)],
                           pg=(prev_gTe.get((s - 1, 2 + h % 2), 0) if h < 2 else 0)):
                    t.wait_ge(c_se.h, nev)
                    key = (s, h - 2) if h >= 2 else (s - 1, h + HEADS - 2)
                    if key in prev_T_evac:
                        t.wait_ge(c_ve.h, prev_T_evac[key])
                    if pg:
                        t.wait_ge(c_ve.h, pg)
                    bank = pp[6 + (h % 2)]
                    last = None
                    for b4 in range(4):
                        last = t.transpose(bank[:, b4 * 65:b4 * 65 + 65],
                                           numsl(h)[0:65, b4 * 128:(b4 + 1) * 128],
                                           idn[0:65, 0:65])
                    c_te.inc(last)
                block.tensor(t_numT)
                numT_done = c_te.n

                def v_tev(vec, h=h, td=numT_done):
                    vec.wait_ge(c_te.h, td)
                    bank = pp[6 + (h % 2)]
                    b3 = bank[:, 0:260].rearrange("p (b j) -> p b j", b=4)
                    vec.reciprocal(rc_sb[:, h * 4:h * 4 + 4], b3[:, :, 64:65].opt())
                    vec.drain()
                    last = None
                    for b4 in range(4):
                        last = vec.tensor_scalar(
                            atm()[:, b4 * 1040 + h * 65:b4 * 1040 + (h + 1) * 65],
                            bank[:, b4 * 65:(b4 + 1) * 65],
                            rc_sb[:, h * 4 + b4:h * 4 + b4 + 1], None, ALU.mult)
                    c_ve.inc(last)
                block.vector(v_tev)
                prev_T_evac[(s, h)] = c_ve.n

            for h in range(HEADS):
                def t_num(t, s=s, h=h):
                    if s == 0 and h == 0:
                        t.wait_ge(s_wl.h, KV_READY + 16)
                        t.wait_ge(c_se.h, Q_DONE)
                    key = (s, h - 2) if h >= 2 else (s - 1, h + HEADS - 2)
                    if key in prev_num_evac:
                        t.wait_ge(c_se.h, prev_num_evac[key])
                    base = (h % 2) * 64
                    bank = pp[4 + (h % 2)]
                    c_te.inc(t.matmul(
                        bank[0:65, :],
                        kvbf[base:base + 64, h * 65:(h + 1) * 65],
                        qbf()[base:base + 64,
                              (h // 2) * T + s * 512:(h // 2) * T + (s + 1) * 512],
                        start=True, stop=True))
                block.tensor(t_num)
                num_done = c_te.n
                prev_num_evac[(s, h)] = c_se.reserve(1)

                def s_nev(sc, s=s, h=h, nd=num_done,
                          gw=(gated_last[0] if (s > 0 and h == 0) else 0)):
                    sc.wait_ge(c_te.h, nd)
                    if gw:
                        sc.wait_ge(c_ve.h, gw)
                    sc.copy(numsl(h)[0:65, :], pp[4 + (h % 2)][0:65, :]).then_inc(c_se.h, 1)
                tgt.add(s_nev)
                if h >= 1:
                    emit_T(h - 1)
                if inject and h in inject:
                    for fn in inject[h]:
                        fn()
            emit_T(HEADS - 1)
            strip_T_done[s] = prev_T_evac[(s, HEADS - 1)]

        pre_th = {}
        pre_vh = {}
        for pbi in range(2):
            def t_h_pre(t, bi=pbi):
                if bi == 0:
                    t.wait_ge(c_se.h, se_qevac[NCH - 2])
                    t.wait_ge(c_ve.h, CB_DONE)
                pb0, pb1 = 2 * bi, 2 * bi + 1
                last = None
                for c in range(NCH):
                    lhs = lnc_view[:, c, bi * 128:(bi + 1) * 128]
                    t.matmul(pp[pb0][:, :], lhs, wh_bf(c)[:, 0:512],
                             start=(c == 0), stop=(c == NCH - 1))
                    last = t.matmul(pp[pb1][:, :], lhs, wh_bf(c)[:, 512:1024],
                                    start=(c == 0), stop=(c == NCH - 1))
                c_te.inc(last)
            block.tensor(t_h_pre)
            pre_th[pbi] = c_te.n

            def v_h_pre(vec, bi=pbi, hd=pre_th[pbi]):
                vec.wait_ge(c_te.h, hd)
                pb0, pb1 = 2 * bi, 2 * bi + 1
                vec.tensor_tensor(hpc(bi)[:, 0:512], pp[pb0][:, :],
                                  cb_b[:, 0:512], ALU.add)
                c_ve.inc(vec.tensor_tensor(hpc(bi)[:, 512:1024], pp[pb1][:, :],
                                           cb_b[:, 512:1024], ALU.add))
            block.vector(v_h_pre)
            pre_vh[pbi] = c_ve.n

        emit_num_phase(0)
        for s in range(NSTRIP):
            blks = [s * 4 + j for j in range(4)]

            vh_l = {}
            sc3_box = [None]

            def mk_t_h(bi):
                b = blks[bi]

                def t_h(t, s=s, b=b, bi=bi,
                        pl=(prev_lnTe.get((s - 1, 3), 0) if (bi == 0 and s > 0) else 0),
                        pa=(ao1_done_l.get((s - 1, 3), 0) if (bi == 1 and s > 0) else 0),
                        vh=vh_l.get(bi - 2, 0)):
                    if s == 0 and bi == 0:
                        t.wait_ge(c_se.h, Q_DONE)
                        t.wait_ge(c_ve.h, CB_DONE)
                    if pl:
                        t.wait_ge(c_ve.h, pl)
                    if pa:
                        t.wait_ge(c_se.h, pa)
                    if vh:
                        t.wait_ge(c_ve.h, vh)
                    p0, p1 = (0, 1) if bi % 2 == 0 else (2, 3)
                    last = None
                    for c in range(NCH):
                        lhs = lnc_view[:, c, b * 128:(b + 1) * 128]
                        t.matmul(pp[p0][:, :], lhs, wh_bf(c)[:, 0:512],
                                 start=(c == 0), stop=(c == NCH - 1))
                        last = t.matmul(pp[p1][:, :], lhs, wh_bf(c)[:, 512:1024],
                                        start=(c == 0), stop=(c == NCH - 1))
                    c_te.inc(last)
                block.tensor(t_h)
                return c_te.n

            def mk_v_h(bi, hd):
                b = blks[bi]

                def v_h(vec, b=b, bi=bi, hd=hd):
                    p0, p1 = (0, 1) if bi % 2 == 0 else (2, 3)
                    vec.wait_ge(c_te.h, hd)
                    vec.tensor_tensor(hpc(b)[:, 0:512], pp[p0][:, :], cb_b[:, 0:512], ALU.add)
                    c_ve.inc(vec.tensor_tensor(hpc(b)[:, 512:1024], pp[p1][:, :],
                                               cb_b[:, 512:1024], ALU.add))
                block.vector(v_h)
                vh_l[bi] = c_ve.n
                return c_ve.n

            th0 = pre_th[0] if s == 0 else mk_t_h(0)
            th1 = pre_th[1] if s == 0 else mk_t_h(1)
            if s == 0:
                vh_l.update(pre_vh)

            def vA(vec, s=s):
                vec.drain()
                for bi in range(4):
                    at3 = atm()[:, bi * 1040:bi * 1040 + 1040].rearrange(
                        "p (h j) -> p h j", h=HEADS)
                    vec.reduce_sum(sp1[:, bi:bi + 1], at3[:, :, 0:64], axis=AXY)
                c_ve.inc(vec.drain())
            block.vector(vA)
            DIVRED = c_ve.n
            if s > 0:
                mk_v_h(0, th0)
                mk_v_h(1, th1)
            th2 = mk_t_h(2)
            th3 = mk_t_h(3)
            for f in dsc2.fns:
                dsc.add(f)
            dsc2.fns = []
            ASS_DONE = c_se.reserve(1)

            def sA(sc, s=s, dr=DIVRED):
                sc.wait_ge(c_ve.h, dr)
                last = None
                for bi in range(4):
                    at_str = atm()[:, bi * 1040:bi * 1040 + 1040].rearrange(
                        "p (h j) -> p h j", h=HEADS)[:, :, 0:64]
                    last = sc.activation(
                        gfm()[:, 0:1024].rearrange("p (h j) -> p h j", h=HEADS),
                        at_str, ACT.Square, bias=0.0, scale=1.0,
                        accum_out=sp1[:, 4 + bi:5 + bi])
                last.then_inc(c_se.h, 1)
            dsc.add(sA)

            def vB(vec):
                vec.wait_ge(c_se.h, ASS_DONE)
                vec.drain()
                vec.tensor_scalar(sp1[:, 8:12], sp1[:, 4:8], 1.0 / HID, None, ALU.mult)
                vec.tensor_scalar(sp1[:, 0:4], sp1[:, 0:4], 1.0 / HID, None, ALU.mult)
                vec.drain()
                vec.tensor_tensor(sp1[:, 12:16], sp1[:, 0:4], sp1[:, 0:4], ALU.mult)
                vec.drain()
                c_ve.inc(vec.scalar_tensor_tensor(sp1[:, 8:12], sp1[:, 8:12], EPS,
                                                  sp1[:, 12:16], ALU.add, ALU.subtract))
            block.vector(vB)
            AVAR_DONE = c_ve.n
            mk_v_h(2, th2)
            ASD_DONE = c_se.reserve(1)

            def sB(sc, av=AVAR_DONE):
                sc.wait_ge(c_ve.h, av)
                sc.activation(sp1[:, 12:16], sp1[:, 8:12], ACT.Sqrt,
                              bias=0.0, scale=1.0).then_inc(c_se.h, 1)
            dsc.add(sB)

            def vC(vec):
                vec.wait_ge(c_se.h, ASD_DONE)
                vec.reciprocal(sp1[:, 16:20], sp1[:, 12:16])
                vec.drain()
                c_ve.inc(vec.scalar_tensor_tensor(sp1[:, 20:24], sp1[:, 0:4], -1.0,
                                                  sp1[:, 16:20], ALU.mult, ALU.mult))
            block.vector(vC)
            ASTATS = c_ve.n
            mk_v_h(3, th3)
            prev_h_evac = vh_l[3]

            gT_done_l = {}
            wo_done_l = {}

            def mk_ao1(bi):
                boff = bi * 1040
                ao1_done_l[(s, bi)] = c_se.reserve(1)

                def s_ao1(sc, bi=bi, wd=wo_done_l[bi], boff=boff):
                    sc.wait_ge(c_te.h, wd)
                    sc.copy(atm()[:, boff:boff + 512], pp[2][:, :])
                    sc.copy(atm()[:, boff + 512:boff + 1024],
                            pp[3][:, :]).then_inc(c_se.h, 1)
                dsc.add(s_ao1)

            def mk_r2m(bi):
                b = blks[bi]
                boff = bi * 1040

                def v_r2(vec, b=b, boff=boff, ad=ao1_done_l[(s, bi)]):
                    vec.wait_ge(c_se.h, ad)
                    xb = x_tm[:, b * HID:(b + 1) * HID]
                    c_ve.inc(vec.scalar_tensor_tensor(
                        xb, atm()[:, boff:boff + 1024], 0.0, xb, ALU.max, ALU.add))
                block.vector(v_r2)
                ve_r2[b] = c_ve.n

                def v_mean(vec, b=b, bi=bi):
                    vec.drain()
                    c_ve.inc(vec.reduce_sum(sp1[:, 32 + bi:33 + bi],
                                            x_tm[:, b * HID:(b + 1) * HID], axis=AX))
                block.vector(v_mean)

                def sC_b(sc, b=b, bi=bi, rw=ve_r2[b], inc=(bi == 3)):
                    sc.wait_ge(c_ve.h, rw)
                    dump = (ap_f32(A_GFM, 128, 1024) if bi == 3
                            else atm()[:, bi * 1040:bi * 1040 + 1024])
                    a = sc.activation(dump, x_tm[:, b * HID:(b + 1) * HID],
                                      ACT.Square, bias=0.0, scale=1.0,
                                      accum_out=sp1[:, 36 + bi:37 + bi])
                    if inc:
                        a.then_inc(s_ls.h, 1)
                if bi == 3:
                    sc3_box[0] = sC_b
                else:
                    dsc.add(sC_b)

            for bi, b in enumerate(blks):
                boff = bi * 1040
                at_str = atm()[:, boff:boff + 1040].rearrange(
                    "p (h j) -> p h j", h=HEADS)[:, :, 0:64]

                def v_apply_gate(vec, s=s, b=b, bi=bi, at_str=at_str,
                                 gtp=gT_done_l.get(bi - 2, 0)):
                    if gtp:
                        vec.wait_ge(c_te.h, gtp)
                    vec.tensor_scalar(at_str, at_str, sp1[:, 16 + bi:17 + bi],
                                      sp1[:, 20 + bi:21 + bi], ALU.mult, ALU.add)
                    vec.drain()
                    c_ve.inc(vec.tensor_tensor(
                        gtm(bi)[:, :].rearrange("p (h j) -> p h j", h=HEADS),
                        at_str,
                        hpc(b)[:].rearrange("p (h j) -> p h j", h=HEADS),
                        ALU.mult))
                block.vector(v_apply_gate)
                gated_done = c_ve.n
                if bi == 3:
                    gated_last[0] = c_ve.n

                def t_gT(t, s=s, bi=bi, gd=gated_done,
                         ptv=prev_T_evac.get((s, 14 + bi % 2), 0),
                         pge=prev_gTe.get((s, bi - 2), 0)):
                    t.wait_ge(c_ve.h, gd)
                    if bi < 2 and ptv:
                        t.wait_ge(c_ve.h, ptv)
                    if pge:
                        t.wait_ge(c_ve.h, pge)
                    bank = pp[6 + bi % 2].ap().bitcast(BF16)
                    last = None
                    for c in range(NCH):
                        last = t.transpose(bank[:, c * 128:(c + 1) * 128],
                                           gtm(bi)[:, c * 128:(c + 1) * 128], idn_bf[:])
                    c_te.inc(last)
                block.tensor(t_gT)
                gT_done_l[bi] = c_te.n

                if bi > 0:
                    mk_ao1(bi - 1)

                def v_gTe(vec, bi=bi, gd=gT_done_l[bi]):
                    vec.wait_ge(c_te.h, gd)
                    src2 = pp[6 + bi % 2].ap().bitcast(BF16)
                    dst = gfm()[:].rearrange("p (c n) -> p c n", c=NCH)[
                        :, :, bi * 128:(bi + 1) * 128]
                    c_ve.inc(vec.tensor_copy(dst, src2[:].rearrange("p (c n) -> p c n", c=NCH)))
                block.vector(v_gTe)
                prev_gTe[(s, bi)] = c_ve.n
                gTe_done = c_ve.n

                def t_wo(t, s=s, bi=bi, ge=gTe_done, vh=vh_l[3],
                         pa=ao1_done_l.get((s, bi - 1)) or (ao1_done_l.get((s - 1, 3), 0) if bi == 0 else 0)):
                    t.wait_ge(c_ve.h, ge)
                    if s == 0 and bi == 0:
                        t.wait_ge(s_wl.h, WO_LOADED)
                        t.wait_ge(c_se.h, Q_DONE)
                    if bi == 0:
                        t.wait_ge(c_ve.h, vh)
                    if pa:
                        t.wait_ge(c_se.h, pa)
                    gc = gfm()[:].rearrange("p (c n) -> p c n", c=NCH)
                    t.matmul(pp[2][:, :], ones_row[:], bo_b[0:1, 0:512],
                             start=True, stop=False)
                    t.matmul(pp[3][:, :], ones_row[:], bo_b[0:1, 512:1024],
                             start=True, stop=False)
                    last = None
                    for c in range(NCH):
                        lhs = gc[:, c, bi * 128:(bi + 1) * 128]
                        t.matmul(pp[2][:, :], lhs, wo_bf(c)[:, 0:512],
                                 start=False, stop=(c == NCH - 1))
                        last = t.matmul(pp[3][:, :], lhs, wo_bf(c)[:, 512:1024],
                                        start=False, stop=(c == NCH - 1))
                    c_te.inc(last)
                block.tensor(t_wo)
                wo_done_l[bi] = c_te.n
                if bi > 0:
                    mk_r2m(bi - 1)
            mk_ao1(3)
            mk_r2m(3)

            def vE(vec, lst=s + 1):
                vec.wait_ge(s_ls.h, lst)
                vec.drain()
                vec.tensor_scalar(sp1[:, 40:44], sp1[:, 36:40], 1.0 / HID, None, ALU.mult)
                vec.tensor_scalar(sp1[:, 32:36], sp1[:, 32:36], 1.0 / HID, None, ALU.mult)
                vec.drain()
                vec.tensor_tensor(sp1[:, 44:48], sp1[:, 32:36], sp1[:, 32:36], ALU.mult)
                vec.drain()
                c_ve.inc(vec.scalar_tensor_tensor(sp1[:, 40:44], sp1[:, 40:44], EPS,
                                                  sp1[:, 44:48], ALU.add, ALU.subtract))
            block.vector(vE)
            LVAR_DONE = c_ve.n
            SD_TGT = s + 1

            def sD(sc, lv=LVAR_DONE):
                sc.wait_ge(c_ve.h, lv)
                sc.activation(sp1[:, 44:48], sp1[:, 40:44], ACT.Sqrt,
                              bias=0.0, scale=1.0).then_inc(s_sd.h, 1)

            def make_sD():
                s_sd.reserve(1)
                dsc.add(sD)

            def make_sC3():
                s_ls.reserve(1)
                dsc.add(sc3_box[0])

            def vF(vec, sdt=SD_TGT):
                vec.wait_ge(s_sd.h, sdt)
                vec.reciprocal(sp1[:, 48:52], sp1[:, 44:48])
                vec.drain()
                c_ve.inc(vec.scalar_tensor_tensor(sp1[:, 52:56], sp1[:, 32:36], -1.0,
                                                  sp1[:, 48:52], ALU.mult, ALU.mult))
            block.vector(vF)
            LSTATS = c_ve.n

            a2_l = {}
            lnT_l = {}

            def emit_a2(bi, s=s):
                b = blks[bi]

                def v_apply2(vec, b=b, bi=bi,
                             gate=(gT_done_l.get(bi + 2) if bi < 2 else lnT_l.get(bi - 2, 0))):
                    if gate:
                        vec.wait_ge(c_te.h, gate)
                    c_ve.inc(vec.tensor_scalar(gtm(bi)[:, :], x_tm[:, b * HID:(b + 1) * HID],
                                               sp1[:, 48 + bi:49 + bi],
                                               sp1[:, 52 + bi:53 + bi], ALU.mult, ALU.add))
                block.vector(v_apply2)
                a2_l[bi] = c_ve.n

            def emit_lnT(bi, s=s):
                def t_ln2T(t, s=s, bi=bi, ad=a2_l[bi], hv=prev_h_evac,
                           pv=prev_lnTe.get((s, bi - 1), 0)):
                    t.wait_ge(c_ve.h, ad)
                    if bi == 0:
                        t.wait_ge(c_ve.h, hv)
                    if pv:
                        t.wait_ge(c_ve.h, pv)
                    bank = pp[1].ap().bitcast(BF16)
                    last = None
                    for c in range(NCH):
                        last = t.transpose(bank[:, c * 128:(c + 1) * 128],
                                           gtm(bi)[:, c * 128:(c + 1) * 128], idn_bf[:])
                    c_te.inc(last)
                block.tensor(t_ln2T)
                lnT_l[bi] = c_te.n

            def emit_lne(bi, s=s):
                b = blks[bi]

                def v_ln2e(vec, b=b, bi=bi, td=lnT_l[bi]):
                    vec.wait_ge(c_te.h, td)
                    src2 = pp[1].ap().bitcast(BF16)
                    dst = lnc_view[:, :, b * 128:(b + 1) * 128]
                    c_ve.inc(vec.tensor_copy(dst, src2[:].rearrange("p (c n) -> p c n", c=NCH)))
                block.vector(v_ln2e)
                prev_lnTe[(s, bi)] = c_ve.n
                ve_ln2fm[b] = c_ve.n

            emit_a2(0)
            emit_a2(1)
            if s + 1 < NSTRIP:
                emit_num_phase(s + 1, inject={
                    1: [make_sC3],
                    2: [make_sD],
                    4: [lambda: emit_lnT(0), lambda: emit_a2(2)],
                    6: [lambda: emit_lne(0)],
                    8: [lambda: emit_lnT(1), lambda: emit_a2(3)],
                    10: [lambda: emit_lne(1)],
                    12: [lambda: emit_lnT(2)],
                    13: [lambda: emit_lne(2)],
                    14: [lambda: emit_lnT(3)]})
                emit_lne(3)
            else:
                make_sC3()
                make_sD()
                emit_lnT(0)
                emit_a2(2)
                emit_lne(0)
                emit_lnT(1)
                emit_a2(3)
                emit_lne(1)
                emit_lnT(2)
                emit_lne(2)
                emit_lnT(3)
                emit_lne(3)

        STRIPS_TE = c_te.n
        STRIPS_VE = c_ve.n
        STRIPS_SE = c_se.n

        prev_eT_te = 0
        prev_p7_ve = 0
        for s in range(NSTRIP):
            def t_rt(t, s=s, le=ve_ln2fm[NBLK - 1], ph=prev_h_evac,
                     st3=prev_num_evac[(NSTRIP - 1, HEADS - 1)]):
                if s == 0:
                    t.wait_ge(c_ve.h, le)
                    t.wait_ge(c_ve.h, ph)
                    t.wait_ge(c_se.h, st3)
                last = None
                for c in range(NCH):
                    last = t.matmul(pp[4][0:8, :], wrt_bf[:, c * 8:(c + 1) * 8],
                                    lnc_view[:, c, s * 512:(s + 1) * 512],
                                    start=(c == 0), stop=(c == NCH - 1))
                c_te.inc(last)
            block.tensor(t_rt)
            rt_done = c_te.n

            exp_done = c_se.reserve(1)

            def s_exp(sc, rd=rt_done, pe=prev_eT_te, lt=ve_ln2fm[NBLK - 1]):
                sc.wait_ge(c_te.h, rd)
                if pe:
                    sc.wait_ge(c_te.h, pe)
                else:
                    sc.wait_ge(c_ve.h, lt)
                sc.activation(e_sb()[:, :], pp[4][0:8, :], ACT.Exp,
                              bias=0.0, scale=1.0).then_inc(c_se.h, 1)
            dsc.add(s_exp)

            def t_eT(t, ed=exp_done, pv=prev_p7_ve):
                t.wait_ge(c_se.h, ed)
                if pv:
                    t.wait_ge(c_ve.h, pv)
                last = None
                for b4 in range(4):
                    last = t.transpose(pp[5][:, b4 * 8:b4 * 8 + 8],
                                       e_sb()[0:8, b4 * 128:(b4 + 1) * 128], idn[0:8, 0:8])
                c_te.inc(last)
            block.tensor(t_eT)
            eT_done = c_te.n
            prev_eT_te = c_te.n

            def v_p7(vec, s=s, ed=eT_done):
                vec.wait_ge(c_te.h, ed)
                vec.tensor_copy(eT_sb[:, 0:32], pp[5][:, 0:32])
                vec.drain()
                for b4 in range(4):
                    vec.reduce_sum(stat[:, S_T0:S_T0 + 1], eT_sb[:, b4 * 8:(b4 + 1) * 8],
                                   axis=AX)
                    vec.drain()
                    vec.reciprocal(stat[:, S_T1:S_T1 + 1], stat[:, S_T0:S_T0 + 1])
                    vec.drain()
                    vec.tensor_scalar(p7_tm[:, s * 4 + b4:s * 4 + b4 + 1],
                                      eT_sb[:, b4 * 8 + 7:b4 * 8 + 8],
                                      stat[:, S_T1:S_T1 + 1], None, ALU.mult)
                c_ve.inc(vec.drain())
            block.vector(v_p7)
            prev_p7_ve = c_ve.n
        P7_DONE = c_ve.n
        dsc.emit(block)
        if DEBUG:
            def dbg_r2(sync):
                sync.wait_ge(c_ve.h, ve_r2[NBLK - 1])
                s_dbg.inc(sync.dma_start(out=dbg["d_r2"].ap(), in_=x_tm[:]), 16)
                sync.wait_ge(c_ve.h, P7_DONE)
                s_dbg.inc(sync.dma_start(out=dbg["d_p7"].ap(), in_=p7_tm[:]), 16)
            block.sync(dbg_r2)
        DBG_END2A = s_dbg.n
        A2A_TE_DONE = c_te.n
        A2A_VE_DONE = c_ve.n
        A2A_SE_DONE = c_se.n

        NFS = 2
        te_w1_consumed = []
        te_w2_consumed = []
        se_gelu = {}
        ve_yev = {}
        ve_out = [0] * NBLK
        WL_BASE = s_wl.n

        def w1_count(S, m):
            return WL_BASE + 16 * (S * (NMCH + NCH) + m + 1)

        def w2_count(S, j):
            return WL_BASE + 16 * (S * (NMCH + NCH) + NMCH + j + 1)

        for S in range(NFS):
            for m in range(NMCH):
                def t_u(t, S=S, m=m):
                    if S == 0 and m == 0:
                        t.wait_ge(c_ve.h, STRIPS_VE)
                        t.wait_ge(c_se.h, STRIPS_SE)
                    if S > 0 and m == 0:
                        t.wait_ge(c_ve.h, ve_out[S * 8 - 1])
                    t.wait_ge(s_wl.h, w1_count(S, m))
                    key = m - 2
                    if key >= 0:
                        t.wait_ge(c_se.h, se_gelu[(S, key)])
                    elif S > 0:
                        t.wait_ge(c_se.h, se_gelu[(S - 1, key + NMCH)])
                    pk = (pp[0], pp[1]) if m % 2 == 0 else (pp[2], pp[3])
                    last = None
                    for c in range(NCH):
                        lhsT = w1r(m)[:, c * 128:(c + 1) * 128]
                        t.matmul(pk[0][:, :], lhsT,
                                 lnc_view[:, c, S * 1024:S * 1024 + 512],
                                 start=(c == 0), stop=(c == NCH - 1))
                        last = t.matmul(pk[1][:, :], lhsT,
                                        lnc_view[:, c, S * 1024 + 512:S * 1024 + 1024],
                                        start=(c == 0), stop=(c == NCH - 1))
                    c_te.inc(last)
                block.tensor(t_u)
                te_w1_consumed.append(c_te.n)
                u_done = c_te.n
                se_gelu[(S, m)] = c_se.reserve(1)

                def s_gelu(sc, m=m, ud=u_done, dq=(DBG_END2A if DEBUG else 0), S=S):
                    sc.wait_ge(c_te.h, ud)
                    if DEBUG and m == 0 and S == 0:
                        sc.wait_ge(s_dbg.h, dq)
                    pk = (pp[0], pp[1]) if m % 2 == 0 else (pp[2], pp[3])
                    sc.activation(ug(m)[:, 0:512], pk[0][:, :], ACT.Gelu, bias=0.0, scale=1.0)
                    sc.activation(ug(m)[:, 512:1024], pk[1][:, :], ACT.Gelu,
                                  bias=0.0, scale=1.0).then_inc(c_se.h, 1)
                dsc.add(s_gelu)
            GELU_ALL = se_gelu[(S, NMCH - 1)]

            for j in range(NCH):
                def t_y(t, S=S, j=j, ga=GELU_ALL):
                    t.wait_ge(s_wl.h, w2_count(S, j))
                    t.wait_ge(c_se.h, ga)
                    if S == 0 and j == 0:
                        t.wait_ge(c_ve.h, P7_DONE)
                    key = j - 2
                    if key >= 0:
                        t.wait_ge(c_ve.h, ve_yev[(S, key)])
                    elif S > 0:
                        t.wait_ge(c_ve.h, ve_yev[(S - 1, key + NCH)])
                    pk = (pp[4], pp[5]) if j % 2 == 0 else (pp[6], pp[7])
                    last = None
                    for m in range(NMCH):
                        lhsT = w2r(j)[:, m * 128:(m + 1) * 128]
                        t.matmul(pk[0][:, :], lhsT, ug(m)[:, 0:512],
                                 start=(m == 0), stop=(m == NMCH - 1))
                        last = t.matmul(pk[1][:, :], lhsT, ug(m)[:, 512:1024],
                                        start=(m == 0), stop=(m == NMCH - 1))
                    c_te.inc(last)
                block.tensor(t_y)
                te_w2_consumed.append(c_te.n)
                y_done = c_te.n

                def v_yev(vec, j=j, yd=y_done):
                    vec.wait_ge(c_te.h, yd)
                    pk = (pp[4], pp[5]) if j % 2 == 0 else (pp[6], pp[7])
                    vec.tensor_copy(yfm(j)[:, 0:512], pk[0][:, :])
                    c_ve.inc(vec.tensor_copy(yfm(j)[:, 512:1024], pk[1][:, :]))
                block.vector(v_yev)
                ve_yev[(S, j)] = c_ve.n
            Y_ALL = ve_yev[(S, NCH - 1)]
            if DEBUG and S == 0:
                def dbg_y(sync, ya=Y_ALL):
                    sync.wait_ge(c_ve.h, ya)
                    s_dbg.inc(sync.dma_start(out=dbg["d_y"].ap(),
                                             in_=ap_bf16(A_YFM, 128, NCH * 1024)), 16)
                block.sync(dbg_y)
            DBG_Y = s_dbg.n

            for bi in range(8):
                gblk = S * 8 + bi

                def t_yT(t, S=S, bi=bi, ya=Y_ALL, gblk=gblk, ga=GELU_ALL):
                    t.wait_ge(c_ve.h, ya)
                    t.wait_ge(c_se.h, ga)
                    if gblk >= 2:
                        t.wait_ge(c_ve.h, ve_out[gblk - 2])
                    bank = pp[gblk % 2].ap().bitcast(BF16)
                    last = None
                    for j in range(NCH):
                        last = t.transpose(bank[:, j * 128:(j + 1) * 128],
                                           yfm(j)[:, bi * 128:(bi + 1) * 128], idn_bf[:])
                    c_te.inc(last)
                block.tensor(t_yT)
                yT_done = c_te.n

                def v_out(vec, gblk=gblk, yd=yT_done, dgy=(DBG_Y if DEBUG else 0)):
                    vec.wait_ge(c_te.h, yd)
                    vec.wait_ge(c_ve.h, P7_DONE)
                    if DEBUG:
                        vec.wait_ge(s_dbg.h, dgy)
                    bank = pp[gblk % 2].ap().bitcast(BF16)
                    xb = x_tm[:, gblk * HID:(gblk + 1) * HID]
                    c_ve.inc(vec.scalar_tensor_tensor(
                        xb, bank[:, 0:1024], p7_tm[:, gblk:gblk + 1], xb, ALU.mult, ALU.add))
                block.vector(v_out)
                ve_out[gblk] = c_ve.n

            def sy_store(sync, S=S, vo1=ve_out[S * 8 + 3], vo2=ve_out[S * 8 + 7]):
                orr = out_d.ap().rearrange("(t p) k -> p t k", p=128)
                xrr = x_tm[:].rearrange("p (t k) -> p t k", t=NBLK)
                sync.wait_ge(c_ve.h, vo1)
                s_st.inc(sync.dma_start(out=orr[:, S * 8:S * 8 + 4, :],
                                        in_=xrr[:, S * 8:S * 8 + 4, :]), 16)
                sync.wait_ge(c_ve.h, vo2)
                s_st.inc(sync.dma_start(out=orr[:, S * 8 + 4:S * 8 + 8, :],
                                        in_=xrr[:, S * 8 + 4:S * 8 + 8, :]), 16)
            block.sync(sy_store)

        dsc.emit(block)

        def gp_ffn(gp):
            gp.wait_ge(c_te.h, STRIPS_TE)
            gp.wait_ge(c_ve.h, STRIPS_VE)
            gp.wait_ge(c_se.h, STRIPS_SE)
            n = 0
            for S in range(NFS):
                for m in range(NMCH):
                    idx = S * NMCH + m
                    if idx >= 4:
                        gp.wait_ge(c_te.h, te_w1_consumed[idx - 4])
                    gp.dma_start(
                        out=w1r(m).rearrange("p (c j) -> p c j", c=NCH),
                        in_=w1_d.ap()[:, m * 128:(m + 1) * 128].rearrange(
                            "(c k) j -> k c j", c=NCH)).then_inc(s_wl.h, 16)
                    n += 16
                for j in range(NCH):
                    idx = S * NCH + j
                    if idx >= 2:
                        gp.wait_ge(c_te.h, te_w2_consumed[idx - 2])
                    gp.dma_start(
                        out=w2r(j).rearrange("p (mc jj) -> p mc jj", mc=NMCH),
                        in_=w2_d.ap()[:, j * 128:(j + 1) * 128].rearrange(
                            "(mc k) jj -> k mc jj", mc=NMCH)).then_inc(s_wl.h, 16)
                    n += 16
            s_wl.n += 0
        block.gpsimd(gp_ffn)

        def sy_final(sync):
            sync.wait_ge(s_st.h, s_st.n)
        block.sync(sy_final)

    return nc


def _permutation():
    perm = np.empty(HID, dtype=np.int64)
    for fp in range(HID):
        h, d = fp // DH, fp % DH
        perm[fp] = d * HEADS + h
    return perm


def kernel(L0_attn_x, L0_attn_Wh, L0_attn_bh, L0_attn_Wq, L0_attn_bq, L0_attn_Wk,
           L0_attn_bk, L0_attn_Wv, L0_attn_bv, L0_attn_beta, L0_attn_ln_w,
           L0_attn_ln_b, L0_attn_Wo, L0_attn_bo, L0_ffn_Wr, L0_ffn_W1, L0_ffn_b1,
           L0_ffn_W2, L0_ffn_b2):
    perm = _permutation()
    x = np.ascontiguousarray(np.asarray(L0_attn_x, dtype=np.float32).reshape(SEQ, HID))
    wq = np.ascontiguousarray(np.asarray(L0_attn_Wq, np.float32)[:, perm])
    wk = np.ascontiguousarray(np.asarray(L0_attn_Wk, np.float32)[:, perm])
    wv = np.ascontiguousarray(np.asarray(L0_attn_Wv, np.float32)[:, perm])
    wh = np.ascontiguousarray(np.asarray(L0_attn_Wh, np.float32)[:, perm])
    wo = np.ascontiguousarray(np.asarray(L0_attn_Wo, np.float32)[perm, :])
    bh = np.ascontiguousarray(np.asarray(L0_attn_bh, np.float32)[perm].reshape(1, HID))
    beta = np.ascontiguousarray(np.asarray(L0_attn_beta, np.float32)[perm].reshape(1, HID))
    bo = np.ascontiguousarray(np.asarray(L0_attn_bo, np.float32).reshape(1, HID))
    wrt = np.ascontiguousarray(np.asarray(L0_ffn_Wr, np.float32).T)
    w1 = np.ascontiguousarray(np.asarray(L0_ffn_W1, np.float32)[7])
    w2 = np.ascontiguousarray(np.asarray(L0_ffn_W2, np.float32)[7])
    idn = np.eye(128, dtype=np.float32)

    nc = build_nc()
    in_maps = []
    for c in range(NCORES):
        in_maps.append({
            "x": np.ascontiguousarray(x[c * T:(c + 1) * T]),
            "wq": wq, "wk": wk, "wv": wv, "wh": wh, "wo": wo,
            "bh": bh, "beta": beta, "bo": bo, "wrt": wrt,
            "w1": w1, "w2": w2, "idn_h": idn,
        })
    res = run_bass_kernel_spmd(nc, in_maps, core_ids=list(range(NCORES)))
    if DEBUG:
        global DBG_RESULTS
        DBG_RESULTS = res.results
    out = np.concatenate([res.results[c]["out"] for c in range(NCORES)], axis=0)
    return out.reshape(1, SEQ, HID).astype(np.float32)
